# version 47
# baseline (speedup 1.0000x reference)
import numpy as np
import concourse.bass as bass
import concourse.mybir as mybir
from concourse.bass_utils import run_bass_kernel_spmd
from contextlib import ExitStack

F32 = mybir.dt.float32
BF16 = mybir.dt.bfloat16
I32 = mybir.dt.int32
AF = mybir.ActivationFunctionType
ALU = mybir.AluOpType
AX = mybir.AxisListType

T = 2048
NC_ = 4
TQ = 512
D = 1024
DC = 8
NE = 8
EPS = 1e-5


class Sched:

    def __init__(self):
        self.sems = {}
        self.cnt = {}
        self.ev = {}
        self.qrot = {}

    def add(self, name, handle):
        self.sems[name] = handle
        self.cnt[name] = 0

    def inc(self, ins, sem, event=None, k=1):
        self.cnt[sem] += k
        if event is not None:
            self.ev[event] = (self.sems[sem], self.cnt[sem])
        return ins.then_inc(self.sems[sem], k)

    def val(self, sem):
        return self.cnt[sem]

    def fence(self, eng, ins, sem, event=None):
        self.cnt[sem] += 1
        ins.then_inc(self.sems[sem], 1)
        eng.wait_ge(self.sems[sem], self.cnt[sem])
        if event is not None:
            self.ev[event] = (self.sems[sem], self.cnt[sem])

    def rbfence(self, eng, ins, sem, event, copyfn, reads):
        self.cnt[sem] += 1
        ins.then_inc(self.sems[sem], 1)
        eng.wait_ge(self.sems[sem], self.cnt[sem])
        last = None
        for rd in reads:
            if last is not None:
                self.cnt[sem] += 1
                last.then_inc(self.sems[sem], 1)
                eng.wait_ge(self.sems[sem], self.cnt[sem])
            last = copyfn(rd)
        self.cnt[sem] += 1
        last.then_inc(self.sems[sem], 1)
        self.ev[event] = (self.sems[sem], self.cnt[sem])

    def dmaq(self, eng, sem, out, in_, event=None):
        use = sem
        if sem + "1" in self.sems:
            r = self.qrot.get(sem, 0)
            self.qrot[sem] = r + 1
            use = (sem, sem + "1", sem + "2")[r % 3]
        if self.cnt[use] > 0:
            eng.wait_ge(self.sems[use], self.cnt[use])
        ins = eng.dma_start(out=out, in_=in_)
        return self.inc(ins, use, event, 16)

    def wait(self, eng, *events):
        need = {}
        for e in events:
            if e is None:
                continue
            h, v = self.ev[e]
            if need.get(id(h), (None, -1))[1] < v:
                need[id(h)] = (h, v)
        for h, v in need.values():
            eng.wait_ge(h, v)


def build_nc(dbg=False):
    nc = bass.Bass("TRN2", target_bir_lowering=False)

    xT = nc.dram_tensor("xT", [D, T], F32, kind="ExternalInput").ap()
    wk = nc.dram_tensor("wk", [D, D], F32, kind="ExternalInput").ap()
    wv = nc.dram_tensor("wv", [D, D], F32, kind="ExternalInput").ap()
    wr = nc.dram_tensor("wr", [D, D], F32, kind="ExternalInput").ap()
    wo = nc.dram_tensor("wo", [D, D], F32, kind="ExternalInput").ap()
    we = nc.dram_tensor("we", [NE, D, D], F32, kind="ExternalInput").ap()
    wrt = nc.dram_tensor("wrt", [128, DC, NE], F32, kind="ExternalInput").ap()
    brt = nc.dram_tensor("brt", [NE, 1], F32, kind="ExternalInput").ap()
    bex = nc.dram_tensor("bex", [NE, D], F32, kind="ExternalInput").ap()
    td = nc.dram_tensor("td", [128, DC], F32, kind="ExternalInput").ap()
    tf = nc.dram_tensor("tf", [128, DC], F32, kind="ExternalInput").ap()
    out = nc.dram_tensor("out", [D, T], F32, kind="ExternalOutput").ap()
    wec = nc.dram_tensor("wec", [NE, D, D], BF16)
    if dbg:
        d_dcol = nc.dram_tensor("d_dcol", [128, DC], F32, kind="ExternalOutput").ap()
        d_eu = nc.dram_tensor("d_eu", [128, DC], F32, kind="ExternalOutput").ap()
        d_attn = nc.dram_tensor("d_attn", [128, DC * T], BF16, kind="ExternalOutput").ap()
        d_ybf = nc.dram_tensor("d_ybf", [128, DC * T], BF16, kind="ExternalOutput").ap()
        d_lg = nc.dram_tensor("d_lg", [8, T], F32, kind="ExternalOutput").ap()
        d_wfm = nc.dram_tensor("d_wfm", [8, T], BF16, kind="ExternalOutput").ap()
        d_x1 = nc.dram_tensor("d_x1", [128, DC * T], BF16, kind="ExternalOutput").ap()
        d_y2 = nc.dram_tensor("d_y2", [128, DC * TQ], BF16, kind="ExternalOutput").ap()
        d_wb = nc.dram_tensor("d_wb", [128, NE * TQ], BF16, kind="ExternalOutput").ap()
        d_kf = nc.dram_tensor("d_kf", [128, T], F32, kind="ExternalOutput").ap()
        d_isl = nc.dram_tensor("d_isl", [128, 64], F32, kind="ExternalOutput").ap()
        d_vf = nc.dram_tensor("d_vf", [128, T], BF16, kind="ExternalOutput").ap()
        d_rf = nc.dram_tensor("d_rf", [128, T], BF16, kind="ExternalOutput").ap()
        d_xbf = nc.dram_tensor("d_xbf", [128, DC * T], BF16, kind="ExternalOutput").ap()
        d_wkb = nc.dram_tensor("d_wkb", [128, DC * D], BF16, kind="ExternalOutput").ap()
        d_ss = nc.dram_tensor("d_ss", [128, T], F32, kind="ExternalOutput").ap()

    S = Sched()

    with ExitStack() as outer:
        def osb(name, shape, dt=F32):
            return outer.enter_context(nc.sbuf_tensor(name, shape, dt))

        x1bf = osb("x1bf", [128, DC * T], BF16)
        lg_sb = osb("lg_sb", [8, T])
        id_f = osb("id_f", [128, 128])
        td_sb = osb("td_sb", [128, DC])
        tf_sb = osb("tf_sb", [128, DC])
        dcol8 = osb("dcol8", [128, DC])
        eucol8 = osb("eucol8", [128, DC])
        lam8 = osb("lam8", [128, DC])
        neglam8 = osb("neglam8", [128, DC])
        fb_v = osb("fb_v", [128, 16])
        fb_s = osb("fb_s", [128, 16])
        fb_vr = osb("fb_vr", [1, 128])
        fb_sr = osb("fb_sr", [1, 128])
        ones_bf = osb("ones_bf", [128, 1], BF16)
        eps1 = osb("eps1", [1, 1])
        ones_f1 = osb("ones_f1", [1, 128])
        ones_bf1 = osb("ones_bf1", [1, 128], BF16)
        iota_c = osb("iota_c", [128, 128], I32)
        iota_r8 = osb("iota_r8", [8, 128], I32)
        id_bf = osb("id_bf", [128, 128], BF16)
        id_f8 = osb("id_f8", [8, 8])
        sel_bf = osb("sel_bf", [8, NE * 128], BF16)
        wfm = osb("wfm", [8, T], BF16)
        brt_sb = osb("brt_sb", [NE, 1])
        bex_bf = osb("bex_bf", [NE, D], BF16)
        wrt_sb = osb("wrt_sb", [128, DC * NE], BF16)

        for nm in ["gl", "sl", "pe", "ve", "sc", "po", "ou",
                   "gl1", "gl2", "sl1", "sl2", "ou1", "ou2"]:
            S.add(nm, outer.enter_context(nc.semaphore("sem_" + nm)))
        GL, SL, PE, VE, SC, PO, OU = "gl", "sl", "pe", "ve", "sc", "po", "ou"

        block = outer.enter_context(nc.Block())
        pA = outer.enter_context(ExitStack())
        attn_in = pA.enter_context(nc.sbuf_tensor("attn_in", [128, DC * T], BF16))
        wkb = pA.enter_context(nc.sbuf_tensor("wkb", [128, DC * D], BF16))

        @block.gpsimd
        def _(g):
            pass

        @block.sync
        def _(sy):
            S.dmaq(sy, SL, td_sb[:], td[:, :], "ld_td")
            S.dmaq(sy, SL, tf_sb[:], tf[:, :], "ld_tf")

            S.dmaq(sy, SL, brt_sb[:], brt[:, :], "ld_brt")

        @block.gpsimd
        def _(g):
            g.iota(iota_c[:], [[1, 128]], base=0, channel_multiplier=-1)
            S.inc(g.iota(iota_r8[:], [[0, 128]], base=0, channel_multiplier=1),
                  PO, "iota_done")

        @block.vector
        def _(v):
            v.memset(ones_bf[:], 1.0)
            v.memset(ones_bf1[:], 1.0)
            v.memset(eps1[:], EPS)
            v.memset(ones_f1[:], 1.0)
            S.wait(v, "iota_done")
            v.tensor_scalar(out=id_bf[:], in0=iota_c[:], scalar1=0.0,
                            scalar2=None, op0=ALU.is_equal)
            v.tensor_scalar(out=id_f8[:], in0=iota_c[0:8, 0:8], scalar1=0.0,
                            scalar2=None, op0=ALU.is_equal)
            v.tensor_scalar(out=id_f[:], in0=iota_c[:], scalar1=0.0,
                            scalar2=None, op0=ALU.is_equal)
            for e in range(NE):
                ins = v.tensor_scalar(out=sel_bf[:, e * 128:(e + 1) * 128],
                                      in0=iota_r8[:], scalar1=float(e),
                                      scalar2=None, op0=ALU.is_equal)
            S.inc(ins, VE, "consts_done")

        @block.scalar
        def _(s):
            S.wait(s, "ld_td", "ld_tf")
            ins = s.activation(out=lam8[:], in_=td_sb[:], func=AF.Exp)
            S.fence(s, ins, SC)
            s.activation(out=dcol8[:], in_=lam8[:], func=AF.Exp, scale=-1.0)
            ins = s.activation(out=eucol8[:], in_=tf_sb[:], func=AF.Exp)
            S.rbfence(s, ins, SC, "decay_done",
                      lambda rd, s=s: s.copy(out=fb_s[:, 0:8], in_=rd),
                      [lam8[:], dcol8[:], eucol8[:]])

        @block.vector
        def _(v):
            S.wait(v, "decay_done")
            ins = v.tensor_scalar(out=neglam8[:], in0=lam8[:], scalar1=-1.0,
                                  scalar2=None, op0=ALU.mult)
            S.rbfence(v, ins, VE, "neglam_done",
                      lambda rd, v=v: v.tensor_copy(out=fb_v[:, 0:8], in_=rd),
                      [neglam8[:]])

        if dbg:
            @block.sync
            def _(sy):
                S.wait(sy, "decay_done")
                S.dmaq(sy, SL, d_dcol[:, :], dcol8[:], None)
                S.dmaq(sy, SL, d_eu[:, :], eucol8[:], None)

        with ExitStack() as p1:
            def p1sb(name, shape, dt=F32):
                return p1.enter_context(nc.sbuf_tensor(name, shape, dt))

            xbf = p1sb("xbf", [128, DC * T], BF16)
            iota1p = p1sb("iota1p", [128, 1024])
            wvb = p1sb("wvb", [128, DC * 512], BF16)
            wrb = p1sb("wrb", [128, DC * 512], BF16)
            k_f = p1sb("k_f", [128, T])
            v_f = p1sb("v_f", [128, T], BF16)
            r_f = p1sb("r_f", [128, T], BF16)
            kv_f = p1sb("kv_f", [128, T])
            ss = p1sb("ss", [128, T + 4])
            ssc = p1sb("ssc", [128, 1024], BF16)
            posd = p1sb("posd", [128, 1024], BF16)
            ps_a = p1.enter_context(nc.psum_tensor("ps_a", [128, T], F32))
            ps_b = p1.enter_context(nc.psum_tensor("ps_b", [128, T], F32))

            @block.gpsimd
            def _(g):
                def wgroup(dst, src, e0, e1, nm, ns=DC, pre=None):
                    if pre is not None:
                        S.wait(g, pre)
                    S.dmaq(g, GL,
                           bass.AP(dst, (e0 % ns) * 128,
                                   [[DC * ns * 128, 128], [ns * 128, DC],
                                    [1, (e1 - e0) * 128]]),
                           src[:, e0 * 128:e1 * 128].rearrange(
                               "(dc p) o -> p dc o", p=128), f"{nm}{e0}")
                    for e in range(e0 + 1, e1):
                        S.ev[f"{nm}{e}"] = S.ev[f"{nm}{e0}"]
                wgroup(wkb, wk, 0, 1, "ld_wk")
                S.dmaq(g, GL, bass.AP(xbf, 512, [[DC * T, 128], [T, DC], [1, 512]]),
                       xT[:, 512:1024].rearrange(
                           "(dc p) t -> p dc t", p=128), "ld_xn1")
                S.inc(g.iota(iota1p[:], [[1, 1024]], base=1, channel_multiplier=0,
                             allow_small_or_imprecise_dtypes=True), PO, "iota1p_done")
                wgroup(wvb, wv, 0, 1, "ld_wv", ns=4)
                wgroup(wrb, wr, 0, 1, "ld_wr", ns=4)
                S.dmaq(g, GL, bass.AP(xbf, 1536, [[DC * T, 128], [T, DC], [1, 512]]),
                       xT[:, 1536:2048].rearrange(
                           "(dc p) t -> p dc t", p=128), "ld_xn3")
                wgroup(wkb, wk, 1, 4, "ld_wk")
                wgroup(wvb, wv, 1, 4, "ld_wv", ns=4)
                wgroup(wrb, wr, 1, 4, "ld_wr", ns=4)
                wgroup(wkb, wk, 4, 8, "ld_wk")

            prev = None
            for ncc in (0, 2):
                @block.sync
                def _(sy, ncc=ncc, prev=prev):
                    for half, stg in ((0, k_f), (1, kv_f)):
                        if prev is not None:
                            S.wait(sy, f"c{prev}_{half}")
                        S.dmaq(sy, SL,
                               stg[:].rearrange("p (dc t) -> p dc t", dc=4),
                               xT[half * 512:(half + 1) * 512,
                                  ncc * 512:(ncc + 1) * 512]
                               .rearrange("(dc p) t -> p dc t", p=128),
                               f"xs{ncc}_{half}")

                @block.vector
                def _(v, ncc=ncc):
                    for half, stg in ((0, k_f), (1, kv_f)):
                        S.wait(v, f"xs{ncc}_{half}")
                        ins = v.tensor_copy(
                            out=bass.AP(xbf, half * 4 * T + ncc * 512,
                                        [[DC * T, 128], [T, 4], [1, 512]]),
                            in_=stg[:].rearrange("p (dc t) -> p dc t", dc=4))
                        S.inc(ins, VE, f"c{ncc}_{half}")
                    S.ev[f"ld_xn{ncc}"] = S.ev[f"c{ncc}_1"]
                prev = ncc

            @block.vector
            def _(v):
                v.memset(ss[:, 0:1], 0.0)

            def proj_mms(t, wtile, ec, psum, ev_pref, wait_evs, ns=DC):
                co = (ec % ns) * 128
                if ec == 0:
                    for ncc in range(NC_):
                        S.wait(t, *wait_evs(0))
                        S.wait(t, f"ld_xn{ncc}")
                        for dc in range(DC):
                            lhsT = wtile[:, dc * ns * 128 + co: dc * ns * 128 + co + 128]
                            rhs = xbf[:, dc * T + ncc * 512: dc * T + (ncc + 1) * 512]
                            ins = t.matmul(psum[:, ncc * 512:(ncc + 1) * 512],
                                           lhsT, rhs,
                                           start=(dc == 0), stop=(dc == DC - 1))
                else:
                    for dc in range(DC):
                        S.wait(t, *wait_evs(dc))
                        lhsT = wtile[:, dc * ns * 128 + co: dc * ns * 128 + co + 128]
                        for ncc in range(NC_):
                            rhs = xbf[:, dc * T + ncc * 512: dc * T + (ncc + 1) * 512]
                            ins = t.matmul(psum[:, ncc * 512:(ncc + 1) * 512],
                                           lhsT, rhs,
                                           start=(dc == 0), stop=(dc == DC - 1))
                S.inc(ins, PE, f"{ev_pref}{ec}")

            for ec in range(DC):
                pa = ps_a if ec % 2 == 0 else ps_b
                pb = ps_b if ec % 2 == 0 else ps_a

                @block.tensor
                def _(t, ec=ec, pa=pa, pb=pb):
                    evs = (lambda dc: (["ld_wk0"] if dc == 0 else [])) if ec == 0 \
                        else (lambda dc, e=ec: ([f"v_ev{e-1}", f"ld_wk{e}"]
                                                if dc == 0 else []))
                    proj_mms(t, wkb, ec, pa, "pe_k", evs)
                    evs = (lambda dc: (["ld_wv0"] if dc == 0 else [])) if ec == 0 \
                        else (lambda dc, e=ec: ([f"r_ev{e-1}", f"ld_wv{e}"]
                                                if dc == 0 else []))
                    proj_mms(t, wvb, ec, pb, "pe_v", evs, ns=4)

                @block.vector
                def _(v, ec=ec, pa=pa):
                    S.wait(v, f"pe_k{ec}")
                    if ec > 0:
                        S.wait(v, f"attn{ec-1}")
                    ins = v.tensor_copy(out=k_f[:], in_=pa[:])
                    S.inc(ins, VE, f"k_ev{ec}")

                @block.tensor
                def _(t, ec=ec, pa=pa):
                    evs = (lambda dc, e=ec: ([f"k_ev{e}", f"ld_wr{e}"]
                                             if dc == 0 else []))
                    proj_mms(t, wrb, ec, pa, "pe_r", evs, ns=4)

                @block.scalar
                def _(s, ec=ec, pa=pa, pb=pb):
                    S.wait(s, f"pe_v{ec}")
                    if ec > 0:
                        S.wait(s, f"kv{ec-1}")
                    ins = s.copy(out=v_f[:], in_=pb[:])
                    S.inc(ins, SC, f"v_ev{ec}")
                    if ec == 0:
                        S.wait(s, "consts_done", "neglam_done", "iota1p_done")
                    else:
                        S.wait(s, f"attn{ec-1}")
                    ins = s.activation(out=posd[:], in_=iota1p[:], func=AF.Exp,
                                       scale=neglam8[:, ec:ec + 1])
                    S.fence(s, ins, SC, f"posd{ec}")
                    S.wait(s, f"pe_r{ec}")
                    if ec > 0:
                        S.wait(s, f"attn{ec-1}")
                    ins = s.activation(out=r_f[:, 0:T],
                                       in_=pa[:], func=AF.Sigmoid)
                    S.inc(ins, SC, f"r_ev{ec}")

                @block.vector
                def _(v, ec=ec):
                    S.wait(v, f"v_ev{ec}", f"posd{ec}", f"k_ev{ec}")
                    ins = v.tensor_tensor(out=kv_f[:], in0=k_f[:], in1=v_f[:],
                                          op=ALU.mult)
                    S.inc(ins, VE, f"kv{ec}")
                    S.wait(v, f"kv{ec}")
                    dbc = bass.AP(dcol8, ec, [[DC, 128], [0, 1024]])
                    v.tensor_tensor_scan(out=ss[:, 1:1025], data0=dbc,
                                         data1=kv_f[:, 0:1024], initial=0.0,
                                         op0=ALU.mult, op1=ALU.add)
                    ins = v.tensor_tensor_scan(out=ss[:, 1025:T + 1], data0=dbc,
                                         data1=kv_f[:, 1024:T], initial=0.0,
                                         op0=ALU.mult, op1=ALU.add)
                    S.inc(ins, VE, f"scan{ec}")
                    S.wait(v, f"scan{ec}")
                    ins = v.scalar_tensor_tensor(out=ssc[:], in0=posd[:],
                                           scalar=ss[:, 1024:1025],
                                           in1=ss[:, 1025:T + 1],
                                           op0=ALU.mult, op1=ALU.add)
                    S.fence(v, ins, VE)
                    v.scalar_tensor_tensor(out=k_f[:, 0:1025], in0=kv_f[:, 0:1025],
                                           scalar=eucol8[:, ec:ec + 1],
                                           in1=ss[:, 0:1025],
                                           op0=ALU.mult, op1=ALU.add)
                    ins = v.scalar_tensor_tensor(out=k_f[:, 1025:T], in0=kv_f[:, 1025:T],
                                           scalar=eucol8[:, ec:ec + 1],
                                           in1=ssc[:, 0:1023],
                                           op0=ALU.mult, op1=ALU.add)
                    S.fence(v, ins, VE)
                    S.wait(v, f"r_ev{ec}")
                    ins = v.tensor_tensor(out=attn_in[:, ec * T:(ec + 1) * T],
                                          in0=k_f[:], in1=r_f[:, 0:T],
                                          op=ALU.mult)
                    S.inc(ins, VE, f"attn{ec}")

                if ec == 3:
                    @block.gpsimd
                    def _(g):
                        for dst, src, nm, pre in ((wvb, wv, "ld_wv", "pe_v3"),
                                                  (wrb, wr, "ld_wr", "pe_r3")):
                            S.wait(g, pre)
                            S.dmaq(g, GL,
                                   bass.AP(dst, 0,
                                           [[DC * 512, 128], [512, DC], [1, 512]]),
                                   src[:, 512:1024].rearrange(
                                       "(dc p) o -> p dc o", p=128), f"{nm}4")
                            for e in range(5, 8):
                                S.ev[f"{nm}{e}"] = S.ev[f"{nm}4"]

            if dbg:
                @block.sync
                def _(sy):
                    S.wait(sy, "attn7")
                    S.dmaq(sy, SL, d_kf[:, :], k_f[:], None)
                    S.dmaq(sy, SL, d_vf[:, :], v_f[:], None)
                    S.dmaq(sy, SL, d_rf[:, :], r_f[:, 0:T], None)
                    S.dmaq(sy, SL, d_xbf[:, :], xbf[:], None)
                    S.dmaq(sy, SL, d_wkb[:, :], wkb[:], None)
                    S.dmaq(sy, SL, d_ss[:, :], ss[:, 1:T + 1], None)
                    S.dmaq(sy, SL, d_attn[:, :], attn_in[:], "dbg_p1")

            @block.gpsimd
            def _(g):
                S.dmaq(g, GL, bex_bf[:], bex[:, :], "ld_bex")
                S.dmaq(g, GL, wrt_sb[:].rearrange("p (dc e) -> p dc e", dc=DC),
                       wrt[:, :, :], "ld_wrt")
                for ec in range(DC):
                    S.wait(g, f"pe_k{ec}")
                    S.dmaq(g, GL,
                           bass.AP(wkb, ec * 128, [[DC * D, 128], [D, DC], [1, 128]]),
                           wo[:, ec * 128:(ec + 1) * 128].rearrange(
                               "(dc p) o -> p dc o", p=128), f"ld_wo{ec}")

        with ExitStack() as p2:
            def p2sb(name, shape, dt=F32):
                return p2.enter_context(nc.sbuf_tensor(name, shape, dt))

            ybf = p2sb("ybf", [128, DC * T], BF16)
            xb_bf = p2sb("xb_bf", [128, 4 * T], BF16)
            sqbf = p2sb("sqbf", [128, 2 * T], BF16)
            m_all = p2sb("m_all", [1, T], BF16)
            var_all = p2sb("var_all", [1, T])
            sd_all = p2sb("sd_all", [1, T])
            rs_all = p2sb("rs_all", [1, T], BF16)
            mb_sb = p2sb("mb_sb", [128, 2 * 512], BF16)
            rb_sb = p2sb("rb_sb", [128, 2 * 512], BF16)
            x1f = p2sb("x1f", [128, DC * 512], BF16)
            ps_a2 = p2.enter_context(nc.psum_tensor("ps_a2", [128, T], F32))
            ps_b2 = p2.enter_context(nc.psum_tensor("ps_b2", [128, T], F32))


            @block.scalar
            def _(s):
                ins = s.activation(out=fb_sr[0:1, 0:1], in_=eps1[:, 0:1],
                                   func=AF.Ln)
                S.fence(s, ins, SC, "ln_warm")

            for oc in range(DC):
                psu = ps_a2 if oc % 2 == 0 else ps_b2

                @block.gpsimd
                def _(g, oc=oc):
                    if oc == 0:
                        S.wait(g, "pe_r7", "attn7")
                    if oc >= 4:
                        S.wait(g, f"y_ev{oc-4}")
                    S.dmaq(g, GL, xb_bf[:, (oc % 4) * T:(oc % 4 + 1) * T],
                           xT[oc * 128:(oc + 1) * 128, :], f"ld_xb{oc}")

                @block.tensor
                def _(t, oc=oc, psu=psu):
                    if oc >= 2:
                        S.wait(t, f"y_ev{oc-2}")
                    for dc in range(DC):
                        if dc == 0:
                            S.wait(t, f"ld_wo{oc}")
                        if oc == 0 and dc == 0:
                            S.wait(t, "r_ev7", "v_ev7")
                        if oc == 0:
                            S.wait(t, f"attn{dc}")
                        lhsT = wkb[:, dc * D + oc * 128: dc * D + (oc + 1) * 128]
                        for ncc in range(NC_):
                            rhs = attn_in[:, dc * T + ncc * 512: dc * T + (ncc + 1) * 512]
                            ins = t.matmul(psu[:, ncc * 512:(ncc + 1) * 512],
                                           lhsT, rhs,
                                           start=(dc == 0), stop=(dc == DC - 1))
                    S.inc(ins, PE, f"pe_y{oc}")

                @block.vector
                def _(v, oc=oc, psu=psu):
                    S.wait(v, f"pe_y{oc}", f"ld_xb{oc}")
                    ins = v.tensor_tensor(out=ybf[:, oc * T:(oc + 1) * T],
                                          in0=psu[:],
                                          in1=xb_bf[:, (oc % 4) * T:(oc % 4 + 1) * T],
                                          op=ALU.add)
                    S.inc(ins, VE, f"y_ev{oc}")

            @block.gpsimd
            def _(g):
                for e in range(NE):
                    S.dmaq(g, GL, wec.ap()[e, :, :], we[e, :, :], f"wec{e}")

            for dc in range(DC):
                @block.vector
                def _(v, dc=dc):
                    S.wait(v, f"y_ev{dc}")
                    if dc >= 2:
                        S.wait(v, f"pe_sq{dc-2}")
                    ins = v.tensor_tensor(out=sqbf[:, (dc % 2) * T:(dc % 2 + 1) * T],
                                          in0=ybf[:, dc * T:(dc + 1) * T],
                                          in1=ybf[:, dc * T:(dc + 1) * T],
                                          op=ALU.mult)
                    S.inc(ins, VE, f"sq{dc}")

                @block.tensor
                def _(t, dc=dc):
                    if dc == 0:
                        S.wait(t, "y_ev6", "y_ev7", "consts_done")
                    S.wait(t, f"sq{dc}")
                    for ncc in range(NC_):
                        t.matmul(ps_a2[0:1, ncc * 512:(ncc + 1) * 512], ones_bf[:, :],
                                 ybf[:, dc * T + ncc * 512: dc * T + (ncc + 1) * 512],
                                 start=(dc == 0), stop=(dc == DC - 1))
                        ins = t.matmul(ps_b2[0:1, ncc * 512:(ncc + 1) * 512], ones_bf[:, :],
                                       sqbf[:, (dc % 2) * T + ncc * 512:(dc % 2) * T + (ncc + 1) * 512],
                                       start=(dc == 0), stop=(dc == DC - 1))
                    S.inc(ins, PE, f"pe_sq{dc}")

            def emit_q(h):
                @block.vector
                def _(v, h=h):
                    hs = slice(h * 512, (h + 1) * 512)
                    if h == 0:
                        S.wait(v, "pe_sq7")

                    def edge(ins, tag, v=v):
                        S.inc(ins, VE, tag)
                        S.wait(v, tag)
                    edge(v.tensor_scalar(out=m_all[0:1, hs], in0=ps_a2[0:1, hs],
                                         scalar1=1.0 / D, scalar2=None,
                                         op0=ALU.mult), f"ln1_m{h}")
                    edge(v.tensor_tensor(out=sd_all[0:1, hs], in0=m_all[0:1, hs],
                                         in1=m_all[0:1, hs], op=ALU.mult),
                         f"ln1_s{h}")
                    ins = v.scalar_tensor_tensor(out=var_all[0:1, hs],
                                                 in0=ps_b2[0:1, hs],
                                                 scalar=1.0 / D,
                                                 in1=sd_all[0:1, hs],
                                                 op0=ALU.mult, op1=ALU.subtract)
                    S.rbfence(v, ins, VE, f"ln1_var{h}",
                              lambda rd, v=v: v.tensor_copy(out=fb_vr[:], in_=rd),
                              [var_all[0:1, h * 512 + 384:h * 512 + 512],
                               m_all[0:1, h * 512 + 384:h * 512 + 512]])

                @block.scalar
                def _(s, h=h):
                    hs = slice(h * 512, (h + 1) * 512)
                    S.wait(s, f"ln1_var{h}")
                    ins = s.activation(out=sd_all[0:1, hs], in_=var_all[0:1, hs],
                                       func=AF.Ln, bias=eps1[:, 0:1])
                    S.inc(ins, SC, f"ln1_ln{h}")
                    s.wait_ge(S.sems[SC], S.cnt[SC])
                    ins = s.activation(out=rs_all[0:1, hs], in_=sd_all[0:1, hs],
                                       func=AF.Exp, scale=-0.5)
                    S.rbfence(s, ins, SC, f"ln1_rs{h}",
                              lambda rd, s=s: s.copy(out=fb_sr[:], in_=rd),
                              [rs_all[0:1, h * 512 + 384:h * 512 + 512]])

            emit_q(0)
            emit_q(1)
            emit_q(2)
            emit_q(3)

            for ncc in range(NC_):
                @block.tensor
                def _(t, ncc=ncc):
                    cs = slice(ncc * 512, (ncc + 1) * 512)
                    S.wait(t, f"ln1_rs{ncc}", f"ln1_var{ncc}")
                    if ncc < 2:
                        S.wait(t, f"ln1_m{ncc + 2}")
                    if ncc > 0:
                        S.wait(t, f"mb_ev{ncc-1}", f"rb_ev{ncc-1}")
                    b = ncc % 2
                    ins = t.matmul(ps_a2[:, b * 512:(b + 1) * 512], ones_bf1[:, :],
                                   m_all[0:1, cs], start=True, stop=True)
                    S.inc(ins, PE, f"pe_mb{ncc}")
                    ins = t.matmul(ps_a2[:, 1024 + b * 512: 1024 + (b + 1) * 512],
                                   ones_bf1[:, :], rs_all[0:1, cs], start=True, stop=True)
                    S.inc(ins, PE, f"pe_rb{ncc}")

                @block.scalar
                def _(s, ncc=ncc):
                    b = ncc % 2
                    S.wait(s, f"pe_mb{ncc}")
                    ins = s.copy(out=mb_sb[:, b * 512:(b + 1) * 512],
                                 in_=ps_a2[:, b * 512:(b + 1) * 512])
                    S.inc(ins, SC, f"mb_ev{ncc}")
                    S.wait(s, f"pe_rb{ncc}")
                    ins = s.copy(out=rb_sb[:, b * 512:(b + 1) * 512],
                                 in_=ps_a2[:, 1024 + b * 512: 1024 + (b + 1) * 512])
                    S.inc(ins, SC, f"rb_ev{ncc}")

                @block.vector
                def _(v, ncc=ncc):
                    b = ncc % 2
                    S.wait(v, f"mb_ev{ncc}", f"rb_ev{ncc}")
                    if ncc > 0:
                        S.wait(v, f"x1f{ncc-1}_{DC-1}")
                    for dp in range(0, DC, 2):
                        for dc in (dp, dp + 1):
                            ins = v.tensor_tensor(
                                out=x1f[:, dc * 512:(dc + 1) * 512],
                                in0=ybf[:, dc * T + ncc * 512: dc * T + (ncc + 1) * 512],
                                in1=mb_sb[:, b * 512:(b + 1) * 512],
                                op=ALU.subtract)
                        S.fence(v, ins, VE)
                        for dc in (dp, dp + 1):
                            ins = v.tensor_tensor(
                                out=x1bf[:, dc * T + ncc * 512: dc * T + (ncc + 1) * 512],
                                in0=x1f[:, dc * 512:(dc + 1) * 512],
                                in1=rb_sb[:, b * 512:(b + 1) * 512],
                                op=ALU.mult)
                            S.inc(ins, VE, f"x1f{ncc}_{dc}")
                    if ncc == NC_ - 1:
                        S.ev["x1c3_7"] = S.ev[f"x1f{ncc}_{DC-1}"]

                for dc in range(DC):
                    @block.tensor
                    def _(t, ncc=ncc, dc=dc):
                        b = ncc % 2
                        S.wait(t, f"x1f{ncc}_{dc}")
                        if ncc == 0 and dc == 0:
                            S.wait(t, "ld_wrt")
                        if ncc >= 2 and dc == 0:
                            S.wait(t, f"lg_ev{ncc-2}")
                        ins = t.matmul(ps_b2[0:8, b * 512:(b + 1) * 512],
                                       wrt_sb[:, dc * 8:(dc + 1) * 8],
                                       x1bf[:, dc * T + ncc * 512: dc * T + (ncc + 1) * 512],
                                       start=(dc == 0), stop=(dc == DC - 1))
                        S.inc(ins, PE, f"rt{ncc}_{dc}")

                @block.vector
                def _(v, ncc=ncc):
                    b = ncc % 2
                    S.wait(v, f"rt{ncc}_7", "ld_brt")
                    ins = v.tensor_scalar(out=lg_sb[:, ncc * 512:(ncc + 1) * 512],
                                          in0=ps_b2[0:8, b * 512:(b + 1) * 512],
                                          scalar1=brt_sb[:, 0:1], scalar2=None,
                                          op0=ALU.add)
                    S.rbfence(v, ins, VE, f"lg_ev{ncc}",
                              lambda rd, v=v: v.tensor_copy(out=fb_v[0:8, 0:8], in_=rd),
                              [lg_sb[0:8, (ncc + 1) * 512 - 8:(ncc + 1) * 512]])

            if dbg:
                @block.sync
                def _(sy):
                    S.wait(sy, *[f"y_ev{o}" for o in range(DC)])
                    S.dmaq(sy, SL, d_ybf[:, :], ybf[:], None)
                    S.wait(sy, *[f"lg_ev{n}" for n in range(NC_)])
                    S.dmaq(sy, SL, d_lg[:, :], lg_sb[:], None)
                    S.wait(sy, "x1c3_7")
                    S.dmaq(sy, SL, d_x1[:, :], x1bf[:], "dbg_p2")

        pA.close()

        with ExitStack() as p34:
            def p4sb(name, shape, dt=F32):
                return p34.enter_context(nc.sbuf_tensor(name, shape, dt))

            WSLB = 3
            XEB = 3
            wsl = p4sb("wsl", [128, WSLB * DC * D], BF16)
            xe = p4sb("xe", [128, XEB * DC * TQ], BF16)
            wb_all = p4sb("wb_all", [128, NE * T], BF16)
            y2bf = p4sb("y2bf", [128, DC * TQ], BF16)
            sq2 = p4sb("sq2", [128, DC * TQ], BF16)
            m2_row = p4sb("m2_row", [1, TQ], BF16)
            s2_row = p4sb("s2_row", [1, TQ])
            v2_row = p4sb("v2_row", [1, TQ])
            r2_row = p4sb("r2_row", [1, TQ], BF16)
            mb2 = p4sb("mb2", [128, TQ], BF16)
            rb2s = p4sb("rb2s", [128, TQ], BF16)
            pisl = ExitStack()
            def pislsb(name, shape, dt=F32):
                return pisl.enter_context(nc.sbuf_tensor(name, shape, dt))
            lgT = pislsb("lgT", [128, 128])
            wtt = pislsb("wtt", [128, 128])
            m1_16 = pislsb("m1_16", [128, 16])
            m2_16 = pislsb("m2_16", [128, 16])
            den16 = pislsb("den16", [128, 16])
            rec16 = pislsb("rec16", [128, 16])
            eqx = pislsb("eqx", [128, 128])
            mskx = pislsb("mskx", [128, 128])
            e1x = pislsb("e1x", [128, 128])
            expx = pislsb("expx", [128, 128])
            keepx = pislsb("keepx", [128, 128])
            ps_f = [p34.enter_context(nc.psum_tensor(f"ps_f{i}", [128, 512], F32))
                    for i in range(8)]

            lgT3 = lgT[:].rearrange("p (i e) -> p i e", i=16)
            mskx3 = mskx[:].rearrange("p (i e) -> p i e", i=16)
            expx3 = expx[:].rearrange("p (i e) -> p i e", i=16)
            m1b = bass.AP(m1_16, 0, [[16, 128], [1, 16], [0, 8]])
            m2b = bass.AP(m2_16, 0, [[16, 128], [1, 16], [0, 8]])
            recb = bass.AP(rec16, 0, [[16, 128], [1, 16], [0, 8]])

            for i in range(16):
                bk = 4 + (i % 2)

                @block.tensor
                def _(t, i=i, bk=bk):
                    S.wait(t, f"lg_ev{i//4}")
                    if i == 0:
                        S.wait(t, "mb_ev3", "rb_ev3", "lg_ev3", "x1c3_7")
                    if i >= 2:
                        S.wait(t, f"lgT_ev{i-2}")
                    ins = t.transpose(ps_f[bk][:, 0:8],
                                      lg_sb[0:8, i * 128:(i + 1) * 128], id_f8[:, :])
                    S.inc(ins, PE, f"pe_tp{i}")

                @block.scalar
                def _(s, i=i, bk=bk):
                    S.wait(s, f"pe_tp{i}")
                    if i == 0:
                        S.wait(s, "x1c3_7")
                        if dbg:
                            S.wait(s, "dbg_p2")
                    ins = s.copy(out=lgT[:, i * 8:(i + 1) * 8], in_=ps_f[bk][:, 0:8])
                    if i in (7, 15):
                        S.rbfence(s, ins, SC, f"lgT_ev{i}",
                                  lambda rd, s=s, i=i: s.copy(out=fb_s[:, 8:16], in_=rd),
                                  [lgT[:, i * 8:(i + 1) * 8]])
                    else:
                        S.inc(ins, SC, f"lgT_ev{i}")

            for h in range(2):
                hs = slice(h * 64, (h + 1) * 64)
                h8 = slice(h * 8, (h + 1) * 8)

                @block.vector
                def _(v, h=h, hs=hs, h8=h8):
                    lgT3h = lgT[:, hs].rearrange("p (i e) -> p i e", i=8)
                    mskx3h = mskx[:, hs].rearrange("p (i e) -> p i e", i=8)
                    m1bh = bass.AP(m1_16, h * 8, [[16, 128], [1, 8], [0, 8]])
                    S.wait(v, f"lgT_ev{h * 8 + 7}")
                    if h == 0:
                        S.wait(v, "x1c3_7")
                        if dbg:
                            S.wait(v, "dbg_p2")

                    def edge(ins, tag, v=v):
                        S.inc(ins, VE, tag)
                        S.wait(v, tag)
                    edge(v.reduce_max(out=m1_16[:, h8], in_=lgT3h, axis=AX.X),
                         f"isl_m1{h}")
                    ins = v.tensor_tensor(
                        out=e1x[:, hs].rearrange("p (i e) -> p i e", i=8),
                        in0=lgT3h, in1=m1bh, op=ALU.subtract)
                    S.rbfence(v, ins, VE, f"isl_e1{h}",
                              lambda rd, v=v: v.tensor_copy(out=fb_v[:, 0:8], in_=rd),
                              [e1x[:, h * 64 + 56:h * 64 + 64]])
                    edge(v.tensor_tensor(
                        out=eqx[:, hs].rearrange("p (i e) -> p i e", i=8),
                        in0=lgT3h, in1=m1bh, op=ALU.is_equal), f"isl_eq{h}")
                    edge(v.scalar_tensor_tensor(out=mskx[:, hs], in0=eqx[:, hs],
                                                scalar=-1e30, in1=lgT[:, hs],
                                                op0=ALU.mult, op1=ALU.add),
                         f"isl_msk{h}")
                    edge(v.reduce_max(out=m2_16[:, h8], in_=mskx3h, axis=AX.X),
                         f"isl_m2{h}")
                    edge(v.tensor_tensor(
                        out=keepx[:, hs].rearrange("p (i e) -> p i e", i=8),
                        in0=lgT3h,
                        in1=bass.AP(m2_16, h * 8, [[16, 128], [1, 8], [0, 8]]),
                        op=ALU.is_ge), f"isl_keep{h}")

            for h in range(2):
                hs = slice(h * 64, (h + 1) * 64)

                @block.scalar
                def _(s, h=h, hs=hs):
                    S.wait(s, f"isl_e1{h}")
                    ins = s.activation(out=expx[:, hs], in_=e1x[:, hs], func=AF.Exp)
                    S.rbfence(s, ins, SC, f"isl_exp{h}",
                              lambda rd, s=s: s.copy(out=fb_s[:, 8:16], in_=rd),
                              [expx[:, h * 64 + 56:h * 64 + 64]])

            for h in range(2):
                hs = slice(h * 64, (h + 1) * 64)
                h8 = slice(h * 8, (h + 1) * 8)

                @block.vector
                def _(v, h=h, hs=hs, h8=h8):
                    expx3h = expx[:, hs].rearrange("p (i e) -> p i e", i=8)
                    recbh = bass.AP(rec16, h * 8, [[16, 128], [1, 8], [0, 8]])
                    S.wait(v, f"isl_exp{h}")

                    def edge(ins, tag, v=v):
                        S.inc(ins, VE, tag)
                        S.wait(v, tag)
                    edge(v.reduce_sum(out=den16[:, h8], in_=expx3h, axis=AX.X),
                         f"isl_den{h}")
                    edge(v.reciprocal(out=rec16[:, h8], in_=den16[:, h8]),
                         f"isl_rec{h}")
                    edge(v.tensor_tensor(
                        out=e1x[:, hs].rearrange("p (i e) -> p i e", i=8),
                        in0=expx3h, in1=recbh, op=ALU.mult), f"isl_t2{h}")
                    ins = v.tensor_tensor(out=wtt[:, hs], in0=e1x[:, hs],
                                          in1=keepx[:, hs], op=ALU.mult)
                    S.rbfence(v, ins, VE, f"isl_w{h}",
                              lambda rd, v=v: v.tensor_copy(out=fb_v[:, 0:8], in_=rd),
                              [wtt[:, h * 64 + 56:h * 64 + 64]])

            for i in range(16):
                bk = 6 + (i % 2)

                @block.tensor
                def _(t, i=i, bk=bk):
                    S.wait(t, f"isl_w{i // 8}")
                    if i >= 2:
                        S.wait(t, f"wfm_ev{i-2}")
                    ins = t.transpose(ps_f[bk][0:8, 0:128],
                                      wtt[:, i * 8:(i + 1) * 8], id_f[:, :])
                    S.inc(ins, PE, f"pe_tw{i}")

                @block.scalar
                def _(s, i=i, bk=bk):
                    S.wait(s, f"pe_tw{i}")
                    ins = s.copy(out=wfm[0:8, i * 128:(i + 1) * 128],
                                 in_=ps_f[bk][0:8, 0:128])
                    if i % 4 == 3:
                        S.rbfence(s, ins, SC, f"wfm_ev{i}",
                                  lambda rd, s=s: s.copy(out=fb_s[0:8, 0:16], in_=rd),
                                  [wfm[0:8, (i + 1) * 128 - 16:(i + 1) * 128]])
                    else:
                        S.inc(ins, SC, f"wfm_ev{i}")

            for qq in range(4):
                for e in range(NE):
                    bk2 = (qq * NE + e) % 4

                    @block.tensor
                    def _(t, qq=qq, e=e, bk2=bk2):
                        S.wait(t, f"wfm_ev{qq * 4 + 3}")
                        gidx = qq * NE + e
                        if gidx >= 4:
                            S.wait(t, f"wbev{(gidx-4)//NE}_{(gidx-4)%NE}")
                        ins = t.matmul(ps_f[bk2][:, :],
                                       sel_bf[:, e * 128:(e + 1) * 128],
                                       wfm[0:8, qq * TQ:(qq + 1) * TQ],
                                       start=True, stop=True)
                        S.inc(ins, PE, f"pe_wb{qq}_{e}")

                    @block.vector
                    def _(v, qq=qq, e=e, bk2=bk2):
                        S.wait(v, f"pe_wb{qq}_{e}")
                        ins = v.tensor_copy(
                            out=wb_all[:, e * T + qq * TQ: e * T + (qq + 1) * TQ],
                            in_=ps_f[bk2][:, :])
                        S.inc(ins, VE, f"wbev{qq}_{e}")

            if dbg:
                @block.sync
                def _(sy):
                    S.wait(sy, *[f"wfm_ev{i}" for i in range(16)])
                    S.dmaq(sy, SL, d_wfm[:, :], wfm[:], "dbg_p3")

            pisl.close()
            outr = p4sb("outr", [128, 4 * TQ])
            otmp = p4sb("otmp", [128, 2 * TQ], BF16)

            OCROT = [4, 5, 6, 7, 2, 3, 0, 1]

            def emit_ln2_tail_a(q):
                for oc in range(NE):
                    @block.tensor
                    def _(t, q=q, oc=oc):
                        S.wait(t, f"sq2{q}_{oc}")
                        if oc == 0:
                            S.wait(t, f"y2_ev{q}_0", f"y2_ev{q}_1")
                        t.matmul(ps_f[0][0:1, :], ones_bf[:, :],
                                 y2bf[:, oc * TQ:(oc + 1) * TQ],
                                 start=(oc == 0), stop=(oc == NE - 1))
                        ins = t.matmul(ps_f[1][0:1, :], ones_bf[:, :],
                                       sq2[:, oc * TQ:(oc + 1) * TQ],
                                       start=(oc == 0), stop=(oc == NE - 1))
                        S.inc(ins, PE, f"st2{q}_{oc}")

                @block.vector
                def _(v, q=q):
                    S.wait(v, f"st2{q}_{NE-1}")

                    def edge(ins, tag, v=v, q=q):
                        S.inc(ins, VE, f"{tag}e{q}")
                        S.wait(v, f"{tag}e{q}")
                    edge(v.tensor_scalar(out=m2_row[:], in0=ps_f[0][0:1, :],
                                         scalar1=1.0 / D, scalar2=None,
                                         op0=ALU.mult), "m2")
                    edge(v.tensor_tensor(out=s2_row[:], in0=m2_row[:], in1=m2_row[:],
                                         op=ALU.mult), "s2")
                    ins = v.scalar_tensor_tensor(out=v2_row[:], in0=ps_f[1][0:1, :],
                                                 scalar=1.0 / D, in1=s2_row[:],
                                                 op0=ALU.mult, op1=ALU.subtract)
                    S.rbfence(v, ins, VE, f"st_ev{q}",
                              lambda rd, v=v: v.tensor_copy(out=fb_vr[:], in_=rd),
                              [v2_row[0:1, 384:512], m2_row[0:1, 384:512]])

                @block.scalar
                def _(s, q=q):
                    S.wait(s, f"st_ev{q}")
                    ins = s.activation(out=s2_row[:], in_=v2_row[:], func=AF.Ln,
                                       bias=eps1[:, 0:1])
                    S.inc(ins, SC, f"ln2l{q}")
                    s.wait_ge(S.sems[SC], S.cnt[SC])
                    ins = s.activation(out=r2_row[:], in_=s2_row[:], func=AF.Exp,
                                       scale=-0.5)
                    S.rbfence(s, ins, SC, f"rs2_ev{q}",
                              lambda rd, s=s: s.copy(out=fb_sr[:], in_=rd),
                              [r2_row[0:1, 384:512]])

            def emit_ln2_bcast(q):
                @block.tensor
                def _(t, q=q):
                    S.wait(t, f"rs2_ev{q}", f"st_ev{q}")
                    ins = t.matmul(ps_f[2][:, :], ones_bf1[:, :], m2_row[0:1, :],
                                   start=True, stop=True)
                    S.inc(ins, PE, f"pe_m2{q}")
                    ins = t.matmul(ps_f[3][:, :], ones_bf1[:, :], r2_row[0:1, :],
                                   start=True, stop=True)
                    S.inc(ins, PE, f"pe_r2{q}")

                @block.scalar
                def _(s, q=q):
                    S.wait(s, f"pe_m2{q}")
                    if q > 0:
                        S.wait(s, f"o_ev{q-1}_{NE-1}")
                    ins = s.copy(out=mb2[:], in_=ps_f[2][:, :])
                    S.inc(ins, SC, f"mb2_ev{q}")
                    S.wait(s, f"pe_r2{q}")
                    ins = s.copy(out=rb2s[:], in_=ps_f[3][:, :])
                    S.inc(ins, SC, f"rb2_ev{q}")

            def emit_ln2_apply(q):
                qs = slice(q * TQ, (q + 1) * TQ)
                for oc in range(NE):
                    gi = q * NE + oc
                    ti = gi % 2
                    ri = gi % 4

                    @block.vector
                    def _(v, q=q, oc=oc, ti=ti, gi=gi):
                        S.wait(v, f"mb2_ev{q}", f"rb2_ev{q}")
                        if gi >= 2:
                            S.wait(v, f"o_ev{(gi-2)//NE}_{(gi-2)%NE}")
                        ins = v.tensor_tensor(out=otmp[:, ti * TQ:(ti + 1) * TQ],
                                              in0=y2bf[:, oc * TQ:(oc + 1) * TQ],
                                              in1=mb2[:], op=ALU.subtract)
                        S.inc(ins, VE, f"o1_{q}_{oc}")

                    @block.vector
                    def _(v, q=q, oc=oc, ti=ti, ri=ri, gi=gi):
                        S.wait(v, f"o1_{q}_{oc}")
                        if gi >= 4:
                            S.wait(v, f"od{(gi-4)//NE}_{(gi-4)%NE}")
                        ins = v.tensor_tensor(out=outr[:, ri * TQ:(ri + 1) * TQ],
                                              in0=otmp[:, ti * TQ:(ti + 1) * TQ],
                                              in1=rb2s[:], op=ALU.mult)
                        S.inc(ins, VE, f"o_ev{q}_{oc}")

                    if oc % 2 == 0:
                        @block.scalar
                        def _(sc2, q=q, oc=oc, ri=ri):
                            S.wait(sc2, f"o_ev{q}_{oc}")
                            S.dmaq(sc2, OU, out[oc * 128:(oc + 1) * 128, qs],
                                   outr[:, ri * TQ:(ri + 1) * TQ], f"od{q}_{oc}")
                    else:
                        @block.sync
                        def _(sy, q=q, oc=oc, ri=ri):
                            S.wait(sy, f"o_ev{q}_{oc}")
                            S.dmaq(sy, SL, out[oc * 128:(oc + 1) * 128, qs],
                                   outr[:, ri * TQ:(ri + 1) * TQ], f"od{q}_{oc}")

            for q in range(4):
                qs = slice(q * TQ, (q + 1) * TQ)

                def emit_wsl(q, e):
                    eb = (q * NE + e) % WSLB

                    @block.sync
                    def _(sy, q=q, e=e, eb=eb):
                        if q == 0:
                            S.wait(sy, f"wec{e}")
                        if q == 0 and e < WSLB:
                            S.wait(sy, "x1c3_7", "lg_ev3", "rt3_7")
                            if dbg:
                                S.wait(sy, "dbg_p2")
                        if q * NE + e >= WSLB:
                            pe_i = (q * NE + e) - WSLB
                            S.wait(sy, f"ffn{pe_i//NE}_{pe_i%NE}")
                        S.dmaq(sy, SL,
                               wsl[:, eb * DC * D:(eb + 1) * DC * D].rearrange(
                                   "p (dc o) -> p dc o", dc=DC),
                               wec.ap()[e, :, :].rearrange("(dc p) o -> p dc o", p=128),
                               f"wsl{q}_{e}")

                def emit_xe(q, e):
                    xb = (q * NE + e) % XEB

                    @block.vector
                    def _(v, q=q, e=e, xb=xb):
                        S.wait(v, f"wbev{q}_{e}")
                        if q * NE + e >= XEB:
                            pe_i = (q * NE + e) - XEB
                            S.wait(v, f"ffn{pe_i//NE}_{pe_i%NE}")
                        for dc in range(DC):
                            ins = v.tensor_tensor(
                                out=xe[:, (xb * DC + dc) * TQ:(xb * DC + dc + 1) * TQ],
                                in0=x1bf[:, dc * T + q * TQ: dc * T + q * TQ + TQ],
                                in1=wb_all[:, e * T + q * TQ: e * T + (q + 1) * TQ],
                                op=ALU.mult)
                        S.inc(ins, VE, f"xe{q}_{e}")

                def ffn_mms(t, q, e, eb, xb, ocs, ocw=None):
                    ins = None
                    for oc in ocs:
                        if ocw is not None:
                            S.wait(t, *ocw(oc))
                        if e == 0 and q == 0 and oc in (0, 1, 2, 3):
                            S.wait(t, f"wbev{3}_{oc+4}")
                        for dc in range(DC):
                            lhsT = wsl[:, (eb * DC + dc) * D + oc * 128:
                                       (eb * DC + dc) * D + (oc + 1) * 128]
                            ins = t.matmul(ps_f[oc][:, :], lhsT,
                                           xe[:, (xb * DC + dc) * TQ:(xb * DC + dc + 1) * TQ],
                                           start=(e == 0 and dc == 0), stop=False)
                    return ins

                if q == 0:
                    for e in range(NE):
                        emit_wsl(q, e)
                        emit_xe(q, e)

                        @block.tensor
                        def _(t, q=q, e=e):
                            eb = (q * NE + e) % WSLB
                            xb = (q * NE + e) % XEB
                            S.wait(t, f"xe{q}_{e}", f"wsl{q}_{e}")
                            if e == 0:
                                S.wait(t, "lgT_ev15", "wfm_ev15")
                            ins = ffn_mms(t, q, e, eb, xb, OCROT)
                            S.inc(ins, PE, f"ffn{q}_{e}")
                else:
                    NA = 3
                    for e in range(NA):
                        if e >= 2:
                            emit_wsl(q, e)
                            emit_xe(q, e)

                        @block.tensor
                        def _(t, q=q, e=e):
                            eb = (q * NE + e) % WSLB
                            xb = (q * NE + e) % XEB
                            S.wait(t, f"xe{q}_{e}", f"wsl{q}_{e}")
                            ocw = None
                            if e == 0:
                                ocw = lambda oc, q=q: [f"y2_ev{q-1}_{oc}"]
                            ffn_mms(t, q, e, eb, xb, [4, 5, 6, 7], ocw=ocw)

                    emit_ln2_tail_a(q - 1)

                    @block.tensor
                    def _(t, q=q, NA=NA):
                        S.wait(t, f"st_ev{q-1}")
                        eb = (q * NE) % WSLB
                        xb = (q * NE) % XEB
                        ffn_mms(t, q, 0, eb, xb, [0, 1])

                    emit_ln2_bcast(q - 1)

                    @block.tensor
                    def _(t, q=q, NA=NA):
                        S.wait(t, f"mb2_ev{q-1}", f"rb2_ev{q-1}")
                        for e in range(NA):
                            eb = (q * NE + e) % WSLB
                            xb = (q * NE + e) % XEB
                            if e > 0:
                                ffn_mms(t, q, e, eb, xb, [0, 1])
                            ins = ffn_mms(t, q, e, eb, xb, [2, 3])
                            S.inc(ins, PE, f"ffn{q}_{e}")

                    def expert_sec(e):
                        @block.tensor
                        def _(t, q=q, e=e):
                            eb = (q * NE + e) % WSLB
                            xb = (q * NE + e) % XEB
                            S.wait(t, f"xe{q}_{e}", f"wsl{q}_{e}")
                            ins = ffn_mms(t, q, e, eb, xb, OCROT)
                            S.inc(ins, PE, f"ffn{q}_{e}")

                    for e in range(NA, NA + 3):
                        emit_wsl(q, e)
                        emit_xe(q, e)
                    expert_sec(NA)
                    expert_sec(NA + 1)
                    for e in range(NA + 3, NE):
                        emit_wsl(q, e)
                        emit_xe(q, e)

                    emit_ln2_apply(q - 1)

                    for e in range(NA + 2, NE):
                        expert_sec(e)

                if q < 3:
                    for e in range(2):
                        emit_wsl(q + 1, e)
                        emit_xe(q + 1, e)

                for oc in ((4, 5, 6, 7, 0, 1, 2, 3) if q < 3 else
                           (0, 1, 2, 3, 4, 5, 6, 7)):
                    @block.tensor
                    def _(t, q=q, oc=oc):
                        if q == 0 and oc == 0:
                            S.wait(t, "ld_bex")
                        ins = t.matmul(ps_f[oc][:, :], bex_bf[:, oc * 128:(oc + 1) * 128],
                                       wfm[0:8, qs], start=False, stop=True)
                        S.inc(ins, PE, f"bia{q}_{oc}")

                    @block.vector
                    def _(v, q=q, oc=oc):
                        S.wait(v, f"bia{q}_{oc}")
                        if dbg and q == 1 and oc == 0:
                            S.wait(v, "dbg_p4")
                        if q > 0:
                            S.wait(v, f"st2{q-1}_{oc}")
                        ins = v.tensor_tensor(out=y2bf[:, oc * TQ:(oc + 1) * TQ],
                                              in0=ps_f[oc][:, :],
                                              in1=x1bf[:, oc * T + q * TQ: oc * T + q * TQ + TQ],
                                              op=ALU.add)
                        S.inc(ins, VE, f"y2_ev{q}_{oc}")

                    @block.vector
                    def _(v2e, q=q, oc=oc):
                        S.wait(v2e, f"y2_ev{q}_{oc}")
                        if q > 0:
                            S.wait(v2e, f"st2{q-1}_{oc}")
                        ins = v2e.tensor_tensor(out=sq2[:, oc * TQ:(oc + 1) * TQ],
                                                in0=y2bf[:, oc * TQ:(oc + 1) * TQ],
                                                in1=y2bf[:, oc * TQ:(oc + 1) * TQ],
                                                op=ALU.mult)
                        S.inc(ins, VE, f"sq2{q}_{oc}")

                if dbg and q == 0:
                    @block.sync
                    def _(sy):
                        S.wait(sy, f"wbev0_{NE-1}")
                        S.dmaq(sy, SL, d_wb[:, :],
                               wb_all[:].rearrange("p (e t) -> p e t", e=NE)[:, :, 0:TQ],
                               None)
                        S.wait(sy, *[f"y2_ev0_{o}" for o in range(NE)])
                        S.dmaq(sy, SL, d_y2[:, :], y2bf[:], "dbg_p4")

            emit_ln2_tail_a(3)
            emit_ln2_bcast(3)
            emit_ln2_apply(3)

            @block.sync
            def _(sy):
                for qn in (OU, OU + "1", OU + "2", SL, SL + "1", SL + "2"):
                    if S.val(qn) > 0:
                        sy.wait_ge(S.sems[qn], S.val(qn))

    return nc


_NC_CACHE = None


def kernel(L0_attn_x, L0_attn_Wk, L0_attn_Wv, L0_attn_Wr, L0_attn_Wo,
           L0_attn_time_decay, L0_attn_time_first,
           L0_ffn_W_experts, L0_ffn_b_experts, L0_ffn_W_router, L0_ffn_b_router):
    global _NC_CACHE
    B = L0_attn_x.shape[0]
    assert L0_attn_x.shape == (8, T, D)

    wkT = np.ascontiguousarray(np.asarray(L0_attn_Wk, np.float32).T)
    wvT = np.ascontiguousarray(np.asarray(L0_attn_Wv, np.float32).T)
    wrT = np.ascontiguousarray(np.asarray(L0_attn_Wr, np.float32).T)
    woT = np.ascontiguousarray(np.asarray(L0_attn_Wo, np.float32).T)
    weT = np.ascontiguousarray(np.asarray(L0_ffn_W_experts, np.float32).transpose(0, 2, 1))
    wrtH = np.ascontiguousarray(
        np.asarray(L0_ffn_W_router, np.float32).T.reshape(DC, 128, NE).transpose(1, 0, 2))
    brtH = np.ascontiguousarray(np.asarray(L0_ffn_b_router, np.float32).reshape(NE, 1))
    bexH = np.ascontiguousarray(np.asarray(L0_ffn_b_experts, np.float32))
    tdH = np.ascontiguousarray(np.asarray(L0_attn_time_decay, np.float32).reshape(DC, 128).T)
    tfH = np.ascontiguousarray(np.asarray(L0_attn_time_first, np.float32).reshape(DC, 128).T)

    x = np.asarray(L0_attn_x, np.float32)
    in_maps = []
    for b in range(B):
        in_maps.append({
            "xT": np.ascontiguousarray(x[b].T),
            "wk": wkT, "wv": wvT, "wr": wrT, "wo": woT, "we": weT,
            "wrt": wrtH, "brt": brtH, "bex": bexH, "td": tdH, "tf": tfH,
        })

    if _NC_CACHE is None:
        _NC_CACHE = build_nc()
    res = run_bass_kernel_spmd(_NC_CACHE, in_maps, core_ids=list(range(8)))
    outs = np.stack([np.asarray(r["out"], np.float32).T for r in res.results], axis=0)
    return outs


# version 49
# speedup vs baseline: 1.0019x; 1.0019x over previous
import numpy as np
import concourse.bass as bass
import concourse.mybir as mybir
from concourse.bass_utils import run_bass_kernel_spmd
from contextlib import ExitStack

F32 = mybir.dt.float32
BF16 = mybir.dt.bfloat16
I32 = mybir.dt.int32
AF = mybir.ActivationFunctionType
ALU = mybir.AluOpType
AX = mybir.AxisListType

T = 2048
NC_ = 4
TQ = 512
D = 1024
DC = 8
NE = 8
EPS = 1e-5


class Sched:

    def __init__(self):
        self.sems = {}
        self.cnt = {}
        self.ev = {}
        self.qrot = {}

    def add(self, name, handle):
        self.sems[name] = handle
        self.cnt[name] = 0

    def inc(self, ins, sem, event=None, k=1):
        self.cnt[sem] += k
        if event is not None:
            self.ev[event] = (self.sems[sem], self.cnt[sem])
        return ins.then_inc(self.sems[sem], k)

    def val(self, sem):
        return self.cnt[sem]

    def fence(self, eng, ins, sem, event=None):
        self.cnt[sem] += 1
        ins.then_inc(self.sems[sem], 1)
        eng.wait_ge(self.sems[sem], self.cnt[sem])
        if event is not None:
            self.ev[event] = (self.sems[sem], self.cnt[sem])

    def rbfence(self, eng, ins, sem, event, copyfn, reads):
        self.cnt[sem] += 1
        ins.then_inc(self.sems[sem], 1)
        eng.wait_ge(self.sems[sem], self.cnt[sem])
        last = None
        for rd in reads:
            if last is not None:
                self.cnt[sem] += 1
                last.then_inc(self.sems[sem], 1)
                eng.wait_ge(self.sems[sem], self.cnt[sem])
            last = copyfn(rd)
        self.cnt[sem] += 1
        last.then_inc(self.sems[sem], 1)
        self.ev[event] = (self.sems[sem], self.cnt[sem])

    def dmaq(self, eng, sem, out, in_, event=None):
        use = sem
        if sem + "1" in self.sems:
            r = self.qrot.get(sem, 0)
            self.qrot[sem] = r + 1
            use = (sem, sem + "1", sem + "2")[r % 3]
        if self.cnt[use] > 0:
            eng.wait_ge(self.sems[use], self.cnt[use])
        ins = eng.dma_start(out=out, in_=in_)
        return self.inc(ins, use, event, 16)

    def wait(self, eng, *events):
        need = {}
        for e in events:
            if e is None:
                continue
            h, v = self.ev[e]
            if need.get(id(h), (None, -1))[1] < v:
                need[id(h)] = (h, v)
        for h, v in need.values():
            eng.wait_ge(h, v)


def build_nc(dbg=False):
    nc = bass.Bass("TRN2", target_bir_lowering=False)

    xT = nc.dram_tensor("xT", [D, T], F32, kind="ExternalInput").ap()
    wk = nc.dram_tensor("wk", [D, D], F32, kind="ExternalInput").ap()
    wv = nc.dram_tensor("wv", [D, D], F32, kind="ExternalInput").ap()
    wr = nc.dram_tensor("wr", [D, D], F32, kind="ExternalInput").ap()
    wo = nc.dram_tensor("wo", [D, D], F32, kind="ExternalInput").ap()
    we = nc.dram_tensor("we", [NE, D, D], F32, kind="ExternalInput").ap()
    wrt = nc.dram_tensor("wrt", [128, DC, NE], F32, kind="ExternalInput").ap()
    brt = nc.dram_tensor("brt", [NE, 1], F32, kind="ExternalInput").ap()
    bex = nc.dram_tensor("bex", [NE, D], F32, kind="ExternalInput").ap()
    td = nc.dram_tensor("td", [128, DC], F32, kind="ExternalInput").ap()
    tf = nc.dram_tensor("tf", [128, DC], F32, kind="ExternalInput").ap()
    out = nc.dram_tensor("out", [D, T], F32, kind="ExternalOutput").ap()
    wec = nc.dram_tensor("wec", [NE, D, D], BF16)
    if dbg:
        d_dcol = nc.dram_tensor("d_dcol", [128, DC], F32, kind="ExternalOutput").ap()
        d_eu = nc.dram_tensor("d_eu", [128, DC], F32, kind="ExternalOutput").ap()
        d_attn = nc.dram_tensor("d_attn", [128, DC * T], BF16, kind="ExternalOutput").ap()
        d_ybf = nc.dram_tensor("d_ybf", [128, DC * T], BF16, kind="ExternalOutput").ap()
        d_lg = nc.dram_tensor("d_lg", [8, T], F32, kind="ExternalOutput").ap()
        d_wfm = nc.dram_tensor("d_wfm", [8, T], BF16, kind="ExternalOutput").ap()
        d_x1 = nc.dram_tensor("d_x1", [128, DC * T], BF16, kind="ExternalOutput").ap()
        d_y2 = nc.dram_tensor("d_y2", [128, DC * TQ], BF16, kind="ExternalOutput").ap()
        d_wb = nc.dram_tensor("d_wb", [128, NE * TQ], BF16, kind="ExternalOutput").ap()
        d_kf = nc.dram_tensor("d_kf", [128, T], F32, kind="ExternalOutput").ap()
        d_isl = nc.dram_tensor("d_isl", [128, 64], F32, kind="ExternalOutput").ap()
        d_vf = nc.dram_tensor("d_vf", [128, T], BF16, kind="ExternalOutput").ap()
        d_rf = nc.dram_tensor("d_rf", [128, T], BF16, kind="ExternalOutput").ap()
        d_xbf = nc.dram_tensor("d_xbf", [128, DC * T], BF16, kind="ExternalOutput").ap()
        d_wkb = nc.dram_tensor("d_wkb", [128, DC * D], BF16, kind="ExternalOutput").ap()
        d_ss = nc.dram_tensor("d_ss", [128, T], F32, kind="ExternalOutput").ap()

    S = Sched()

    with ExitStack() as outer:
        def osb(name, shape, dt=F32):
            return outer.enter_context(nc.sbuf_tensor(name, shape, dt))

        x1bf = osb("x1bf", [128, DC * T], BF16)
        lg_sb = osb("lg_sb", [8, T])
        id_f = osb("id_f", [128, 128])
        td_sb = osb("td_sb", [128, DC])
        tf_sb = osb("tf_sb", [128, DC])
        dcol8 = osb("dcol8", [128, DC])
        eucol8 = osb("eucol8", [128, DC])
        lam8 = osb("lam8", [128, DC])
        neglam8 = osb("neglam8", [128, DC])
        fb_v = osb("fb_v", [128, 16])
        fb_s = osb("fb_s", [128, 16])
        fb_vr = osb("fb_vr", [1, 128])
        fb_sr = osb("fb_sr", [1, 128])
        ones_bf = osb("ones_bf", [128, 1], BF16)
        eps1 = osb("eps1", [1, 1])
        ones_f1 = osb("ones_f1", [1, 128])
        ones_bf1 = osb("ones_bf1", [1, 128], BF16)
        iota_c = osb("iota_c", [128, 128], I32)
        iota_r8 = osb("iota_r8", [8, 128], I32)
        id_bf = osb("id_bf", [128, 128], BF16)
        id_f8 = osb("id_f8", [8, 8])
        sel_bf = osb("sel_bf", [8, NE * 128], BF16)
        wfm = osb("wfm", [8, T], BF16)
        brt_sb = osb("brt_sb", [NE, 1])
        bex_bf = osb("bex_bf", [NE, D], BF16)
        wrt_sb = osb("wrt_sb", [128, DC * NE], BF16)

        for nm in ["gl", "sl", "pe", "ve", "sc", "po", "ou",
                   "gl1", "gl2", "sl1", "sl2", "ou1", "ou2"]:
            S.add(nm, outer.enter_context(nc.semaphore("sem_" + nm)))
        GL, SL, PE, VE, SC, PO, OU = "gl", "sl", "pe", "ve", "sc", "po", "ou"

        block = outer.enter_context(nc.Block())
        pA = outer.enter_context(ExitStack())
        attn_in = pA.enter_context(nc.sbuf_tensor("attn_in", [128, DC * T], BF16))
        wkb = pA.enter_context(nc.sbuf_tensor("wkb", [128, DC * D], BF16))

        @block.gpsimd
        def _(g):
            pass

        @block.gpsimd
        def _(g):
            g.iota(iota_c[:], [[1, 128]], base=0, channel_multiplier=-1)
            S.inc(g.iota(iota_r8[:], [[0, 128]], base=0, channel_multiplier=1),
                  PO, "iota_done")

        @block.vector
        def _(v):
            v.memset(ones_bf[:], 1.0)
            v.memset(ones_bf1[:], 1.0)
            v.memset(eps1[:], EPS)
            v.memset(ones_f1[:], 1.0)
            S.wait(v, "iota_done")
            v.tensor_scalar(out=id_bf[:], in0=iota_c[:], scalar1=0.0,
                            scalar2=None, op0=ALU.is_equal)
            v.tensor_scalar(out=id_f8[:], in0=iota_c[0:8, 0:8], scalar1=0.0,
                            scalar2=None, op0=ALU.is_equal)
            v.tensor_scalar(out=id_f[:], in0=iota_c[:], scalar1=0.0,
                            scalar2=None, op0=ALU.is_equal)
            for e in range(NE):
                ins = v.tensor_scalar(out=sel_bf[:, e * 128:(e + 1) * 128],
                                      in0=iota_r8[:], scalar1=float(e),
                                      scalar2=None, op0=ALU.is_equal)
            S.inc(ins, VE, "consts_done")

        if dbg:
            @block.sync
            def _(sy):
                S.wait(sy, "decay_done")
                S.dmaq(sy, SL, d_dcol[:, :], dcol8[:], None)
                S.dmaq(sy, SL, d_eu[:, :], eucol8[:], None)

        with ExitStack() as p1:
            def p1sb(name, shape, dt=F32):
                return p1.enter_context(nc.sbuf_tensor(name, shape, dt))

            xbf = p1sb("xbf", [128, DC * T], BF16)
            iota1p = p1sb("iota1p", [128, 1024])
            wvb = p1sb("wvb", [128, DC * 512], BF16)
            wrb = p1sb("wrb", [128, DC * 512], BF16)
            k_f = p1sb("k_f", [128, T])
            v_f = p1sb("v_f", [128, T], BF16)
            r_f = p1sb("r_f", [128, T], BF16)
            kv_f = p1sb("kv_f", [128, T])
            ss = p1sb("ss", [128, T + 4])
            ssc = p1sb("ssc", [128, 1024], BF16)
            posd = p1sb("posd", [128, 1024], BF16)
            ps_a = p1.enter_context(nc.psum_tensor("ps_a", [128, T], F32))
            ps_b = p1.enter_context(nc.psum_tensor("ps_b", [128, T], F32))

            @block.gpsimd
            def _(g):
                def wgroup(dst, src, e0, e1, nm, ns=DC, pre=None):
                    if pre is not None:
                        S.wait(g, pre)
                    S.dmaq(g, GL,
                           bass.AP(dst, (e0 % ns) * 128,
                                   [[DC * ns * 128, 128], [ns * 128, DC],
                                    [1, (e1 - e0) * 128]]),
                           src[:, e0 * 128:e1 * 128].rearrange(
                               "(dc p) o -> p dc o", p=128), f"{nm}{e0}")
                    for e in range(e0 + 1, e1):
                        S.ev[f"{nm}{e}"] = S.ev[f"{nm}{e0}"]
                wgroup(wkb, wk, 0, 1, "ld_wk")
                S.dmaq(g, GL, bass.AP(xbf, 512, [[DC * T, 128], [T, DC], [1, 512]]),
                       xT[:, 512:1024].rearrange(
                           "(dc p) t -> p dc t", p=128), "ld_xn1")
                S.inc(g.iota(iota1p[:], [[1, 1024]], base=1, channel_multiplier=0,
                             allow_small_or_imprecise_dtypes=True), PO, "iota1p_done")
                wgroup(wvb, wv, 0, 1, "ld_wv", ns=4)
                wgroup(wrb, wr, 0, 1, "ld_wr", ns=4)
                S.dmaq(g, GL, bass.AP(xbf, 1536, [[DC * T, 128], [T, DC], [1, 512]]),
                       xT[:, 1536:2048].rearrange(
                           "(dc p) t -> p dc t", p=128), "ld_xn3")
                wgroup(wkb, wk, 1, 4, "ld_wk")
                wgroup(wvb, wv, 1, 4, "ld_wv", ns=4)
                wgroup(wrb, wr, 1, 4, "ld_wr", ns=4)
                wgroup(wkb, wk, 4, 8, "ld_wk")

            prev = None
            for ncc in (0, 2):
                @block.sync
                def _(sy, ncc=ncc, prev=prev):
                    for half, stg in ((0, k_f), (1, kv_f)):
                        if prev is not None:
                            S.wait(sy, f"c{prev}_{half}")
                        S.dmaq(sy, SL,
                               stg[:].rearrange("p (dc t) -> p dc t", dc=4),
                               xT[half * 512:(half + 1) * 512,
                                  ncc * 512:(ncc + 1) * 512]
                               .rearrange("(dc p) t -> p dc t", p=128),
                               f"xs{ncc}_{half}")
                    if ncc == 0:
                        S.dmaq(sy, SL, td_sb[:], td[:, :], "ld_td")
                        S.dmaq(sy, SL, tf_sb[:], tf[:, :], "ld_tf")
                        S.dmaq(sy, SL, brt_sb[:], brt[:, :], "ld_brt")

                @block.vector
                def _(v, ncc=ncc):
                    for half, stg in ((0, k_f), (1, kv_f)):
                        S.wait(v, f"xs{ncc}_{half}")
                        ins = v.tensor_copy(
                            out=bass.AP(xbf, half * 4 * T + ncc * 512,
                                        [[DC * T, 128], [T, 4], [1, 512]]),
                            in_=stg[:].rearrange("p (dc t) -> p dc t", dc=4))
                        S.inc(ins, VE, f"c{ncc}_{half}")
                    S.ev[f"ld_xn{ncc}"] = S.ev[f"c{ncc}_1"]
                prev = ncc

        @block.scalar
        def _(s):
            S.wait(s, "ld_td", "ld_tf")
            ins = s.activation(out=lam8[:], in_=td_sb[:], func=AF.Exp)
            S.fence(s, ins, SC)
            s.activation(out=dcol8[:], in_=lam8[:], func=AF.Exp, scale=-1.0)
            ins = s.activation(out=eucol8[:], in_=tf_sb[:], func=AF.Exp)
            S.rbfence(s, ins, SC, "decay_done",
                      lambda rd, s=s: s.copy(out=fb_s[:, 0:8], in_=rd),
                      [lam8[:], dcol8[:], eucol8[:]])

        @block.vector
        def _(v):
            S.wait(v, "decay_done")
            ins = v.tensor_scalar(out=neglam8[:], in0=lam8[:], scalar1=-1.0,
                                  scalar2=None, op0=ALU.mult)
            S.rbfence(v, ins, VE, "neglam_done",
                      lambda rd, v=v: v.tensor_copy(out=fb_v[:, 0:8], in_=rd),
                      [neglam8[:]])


            @block.vector
            def _(v):
                v.memset(ss[:, 0:1], 0.0)

            def proj_mms(t, wtile, ec, psum, ev_pref, wait_evs, ns=DC):
                co = (ec % ns) * 128
                if ec == 0:
                    for ncc in range(NC_):
                        S.wait(t, *wait_evs(0))
                        S.wait(t, f"ld_xn{ncc}")
                        for dc in range(DC):
                            lhsT = wtile[:, dc * ns * 128 + co: dc * ns * 128 + co + 128]
                            rhs = xbf[:, dc * T + ncc * 512: dc * T + (ncc + 1) * 512]
                            ins = t.matmul(psum[:, ncc * 512:(ncc + 1) * 512],
                                           lhsT, rhs,
                                           start=(dc == 0), stop=(dc == DC - 1))
                else:
                    for dc in range(DC):
                        S.wait(t, *wait_evs(dc))
                        lhsT = wtile[:, dc * ns * 128 + co: dc * ns * 128 + co + 128]
                        for ncc in range(NC_):
                            rhs = xbf[:, dc * T + ncc * 512: dc * T + (ncc + 1) * 512]
                            ins = t.matmul(psum[:, ncc * 512:(ncc + 1) * 512],
                                           lhsT, rhs,
                                           start=(dc == 0), stop=(dc == DC - 1))
                S.inc(ins, PE, f"{ev_pref}{ec}")

            for ec in range(DC):
                pa = ps_a if ec % 2 == 0 else ps_b
                pb = ps_b if ec % 2 == 0 else ps_a

                @block.tensor
                def _(t, ec=ec, pa=pa, pb=pb):
                    evs = (lambda dc: (["ld_wk0"] if dc == 0 else [])) if ec == 0 \
                        else (lambda dc, e=ec: ([f"v_ev{e-1}", f"ld_wk{e}"]
                                                if dc == 0 else []))
                    proj_mms(t, wkb, ec, pa, "pe_k", evs)
                    evs = (lambda dc: (["ld_wv0"] if dc == 0 else [])) if ec == 0 \
                        else (lambda dc, e=ec: ([f"r_ev{e-1}", f"ld_wv{e}"]
                                                if dc == 0 else []))
                    proj_mms(t, wvb, ec, pb, "pe_v", evs, ns=4)

                @block.vector
                def _(v, ec=ec, pa=pa):
                    S.wait(v, f"pe_k{ec}")
                    if ec > 0:
                        S.wait(v, f"attn{ec-1}")
                    ins = v.tensor_copy(out=k_f[:], in_=pa[:])
                    S.inc(ins, VE, f"k_ev{ec}")

                @block.tensor
                def _(t, ec=ec, pa=pa):
                    evs = (lambda dc, e=ec: ([f"k_ev{e}", f"ld_wr{e}"]
                                             if dc == 0 else []))
                    proj_mms(t, wrb, ec, pa, "pe_r", evs, ns=4)

                @block.scalar
                def _(s, ec=ec, pa=pa, pb=pb):
                    S.wait(s, f"pe_v{ec}")
                    if ec > 0:
                        S.wait(s, f"kv{ec-1}")
                    ins = s.copy(out=v_f[:], in_=pb[:])
                    S.inc(ins, SC, f"v_ev{ec}")
                    if ec == 0:
                        S.wait(s, "consts_done", "neglam_done", "iota1p_done")
                    else:
                        S.wait(s, f"attn{ec-1}")
                    ins = s.activation(out=posd[:], in_=iota1p[:], func=AF.Exp,
                                       scale=neglam8[:, ec:ec + 1])
                    S.fence(s, ins, SC, f"posd{ec}")
                    S.wait(s, f"pe_r{ec}")
                    if ec > 0:
                        S.wait(s, f"attn{ec-1}")
                    ins = s.activation(out=r_f[:, 0:T],
                                       in_=pa[:], func=AF.Sigmoid)
                    S.inc(ins, SC, f"r_ev{ec}")

                @block.vector
                def _(v, ec=ec):
                    S.wait(v, f"v_ev{ec}", f"posd{ec}", f"k_ev{ec}")
                    ins = v.tensor_tensor(out=kv_f[:], in0=k_f[:], in1=v_f[:],
                                          op=ALU.mult)
                    S.inc(ins, VE, f"kv{ec}")
                    S.wait(v, f"kv{ec}")
                    dbc = bass.AP(dcol8, ec, [[DC, 128], [0, 1024]])
                    v.tensor_tensor_scan(out=ss[:, 1:1025], data0=dbc,
                                         data1=kv_f[:, 0:1024], initial=0.0,
                                         op0=ALU.mult, op1=ALU.add)
                    ins = v.tensor_tensor_scan(out=ss[:, 1025:T + 1], data0=dbc,
                                         data1=kv_f[:, 1024:T], initial=0.0,
                                         op0=ALU.mult, op1=ALU.add)
                    S.inc(ins, VE, f"scan{ec}")
                    S.wait(v, f"scan{ec}")
                    ins = v.scalar_tensor_tensor(out=ssc[:], in0=posd[:],
                                           scalar=ss[:, 1024:1025],
                                           in1=ss[:, 1025:T + 1],
                                           op0=ALU.mult, op1=ALU.add)
                    S.fence(v, ins, VE)
                    v.scalar_tensor_tensor(out=k_f[:, 0:1025], in0=kv_f[:, 0:1025],
                                           scalar=eucol8[:, ec:ec + 1],
                                           in1=ss[:, 0:1025],
                                           op0=ALU.mult, op1=ALU.add)
                    ins = v.scalar_tensor_tensor(out=k_f[:, 1025:T], in0=kv_f[:, 1025:T],
                                           scalar=eucol8[:, ec:ec + 1],
                                           in1=ssc[:, 0:1023],
                                           op0=ALU.mult, op1=ALU.add)
                    S.fence(v, ins, VE)
                    S.wait(v, f"r_ev{ec}")
                    ins = v.tensor_tensor(out=attn_in[:, ec * T:(ec + 1) * T],
                                          in0=k_f[:], in1=r_f[:, 0:T],
                                          op=ALU.mult)
                    S.inc(ins, VE, f"attn{ec}")

                if ec == 3:
                    @block.gpsimd
                    def _(g):
                        for dst, src, nm, pre in ((wvb, wv, "ld_wv", "pe_v3"),
                                                  (wrb, wr, "ld_wr", "pe_r3")):
                            S.wait(g, pre)
                            S.dmaq(g, GL,
                                   bass.AP(dst, 0,
                                           [[DC * 512, 128], [512, DC], [1, 512]]),
                                   src[:, 512:1024].rearrange(
                                       "(dc p) o -> p dc o", p=128), f"{nm}4")
                            for e in range(5, 8):
                                S.ev[f"{nm}{e}"] = S.ev[f"{nm}4"]

            if dbg:
                @block.sync
                def _(sy):
                    S.wait(sy, "attn7")
                    S.dmaq(sy, SL, d_kf[:, :], k_f[:], None)
                    S.dmaq(sy, SL, d_vf[:, :], v_f[:], None)
                    S.dmaq(sy, SL, d_rf[:, :], r_f[:, 0:T], None)
                    S.dmaq(sy, SL, d_xbf[:, :], xbf[:], None)
                    S.dmaq(sy, SL, d_wkb[:, :], wkb[:], None)
                    S.dmaq(sy, SL, d_ss[:, :], ss[:, 1:T + 1], None)
                    S.dmaq(sy, SL, d_attn[:, :], attn_in[:], "dbg_p1")

            @block.gpsimd
            def _(g):
                S.dmaq(g, GL, bex_bf[:], bex[:, :], "ld_bex")
                S.dmaq(g, GL, wrt_sb[:].rearrange("p (dc e) -> p dc e", dc=DC),
                       wrt[:, :, :], "ld_wrt")
                for ec in range(DC):
                    S.wait(g, f"pe_k{ec}")
                    S.dmaq(g, GL,
                           bass.AP(wkb, ec * 128, [[DC * D, 128], [D, DC], [1, 128]]),
                           wo[:, ec * 128:(ec + 1) * 128].rearrange(
                               "(dc p) o -> p dc o", p=128), f"ld_wo{ec}")

        with ExitStack() as p2:
            def p2sb(name, shape, dt=F32):
                return p2.enter_context(nc.sbuf_tensor(name, shape, dt))

            ybf = p2sb("ybf", [128, DC * T], BF16)
            xb_bf = p2sb("xb_bf", [128, 4 * T], BF16)
            sqbf = p2sb("sqbf", [128, 2 * T], BF16)
            m_all = p2sb("m_all", [1, T], BF16)
            var_all = p2sb("var_all", [1, T])
            sd_all = p2sb("sd_all", [1, T])
            rs_all = p2sb("rs_all", [1, T], BF16)
            mb_sb = p2sb("mb_sb", [128, 2 * 512], BF16)
            rb_sb = p2sb("rb_sb", [128, 2 * 512], BF16)
            x1f = p2sb("x1f", [128, DC * 512], BF16)
            ps_a2 = p2.enter_context(nc.psum_tensor("ps_a2", [128, T], F32))
            ps_b2 = p2.enter_context(nc.psum_tensor("ps_b2", [128, T], F32))


            @block.scalar
            def _(s):
                ins = s.activation(out=fb_sr[0:1, 0:1], in_=eps1[:, 0:1],
                                   func=AF.Ln)
                S.fence(s, ins, SC, "ln_warm")

            for oc in range(DC):
                psu = ps_a2 if oc % 2 == 0 else ps_b2

                @block.gpsimd
                def _(g, oc=oc):
                    if oc == 0:
                        S.wait(g, "pe_r7", "attn7")
                    if oc >= 4:
                        S.wait(g, f"y_ev{oc-4}")
                    S.dmaq(g, GL, xb_bf[:, (oc % 4) * T:(oc % 4 + 1) * T],
                           xT[oc * 128:(oc + 1) * 128, :], f"ld_xb{oc}")

                @block.tensor
                def _(t, oc=oc, psu=psu):
                    if oc >= 2:
                        S.wait(t, f"y_ev{oc-2}")
                    for dc in range(DC):
                        if dc == 0:
                            S.wait(t, f"ld_wo{oc}")
                        if oc == 0 and dc == 0:
                            S.wait(t, "r_ev7", "v_ev7")
                        if oc == 0:
                            S.wait(t, f"attn{dc}")
                        lhsT = wkb[:, dc * D + oc * 128: dc * D + (oc + 1) * 128]
                        for ncc in range(NC_):
                            rhs = attn_in[:, dc * T + ncc * 512: dc * T + (ncc + 1) * 512]
                            ins = t.matmul(psu[:, ncc * 512:(ncc + 1) * 512],
                                           lhsT, rhs,
                                           start=(dc == 0), stop=(dc == DC - 1))
                    S.inc(ins, PE, f"pe_y{oc}")

                @block.vector
                def _(v, oc=oc, psu=psu):
                    S.wait(v, f"pe_y{oc}", f"ld_xb{oc}")
                    ins = v.tensor_tensor(out=ybf[:, oc * T:(oc + 1) * T],
                                          in0=psu[:],
                                          in1=xb_bf[:, (oc % 4) * T:(oc % 4 + 1) * T],
                                          op=ALU.add)
                    S.inc(ins, VE, f"y_ev{oc}")

            @block.gpsimd
            def _(g):
                for e in range(NE):
                    S.dmaq(g, GL, wec.ap()[e, :, :], we[e, :, :], f"wec{e}")

            for dc in range(DC):
                @block.vector
                def _(v, dc=dc):
                    S.wait(v, f"y_ev{dc}")
                    if dc >= 2:
                        S.wait(v, f"pe_sq{dc-2}")
                    ins = v.tensor_tensor(out=sqbf[:, (dc % 2) * T:(dc % 2 + 1) * T],
                                          in0=ybf[:, dc * T:(dc + 1) * T],
                                          in1=ybf[:, dc * T:(dc + 1) * T],
                                          op=ALU.mult)
                    S.inc(ins, VE, f"sq{dc}")

                @block.tensor
                def _(t, dc=dc):
                    if dc == 0:
                        S.wait(t, "y_ev6", "y_ev7", "consts_done")
                    S.wait(t, f"sq{dc}")
                    for ncc in range(NC_):
                        t.matmul(ps_a2[0:1, ncc * 512:(ncc + 1) * 512], ones_bf[:, :],
                                 ybf[:, dc * T + ncc * 512: dc * T + (ncc + 1) * 512],
                                 start=(dc == 0), stop=(dc == DC - 1))
                        ins = t.matmul(ps_b2[0:1, ncc * 512:(ncc + 1) * 512], ones_bf[:, :],
                                       sqbf[:, (dc % 2) * T + ncc * 512:(dc % 2) * T + (ncc + 1) * 512],
                                       start=(dc == 0), stop=(dc == DC - 1))
                    S.inc(ins, PE, f"pe_sq{dc}")

            def emit_q(h):
                @block.vector
                def _(v, h=h):
                    hs = slice(h * 512, (h + 1) * 512)
                    if h == 0:
                        S.wait(v, "pe_sq7")

                    def edge(ins, tag, v=v):
                        S.inc(ins, VE, tag)
                        S.wait(v, tag)
                    edge(v.tensor_scalar(out=m_all[0:1, hs], in0=ps_a2[0:1, hs],
                                         scalar1=1.0 / D, scalar2=None,
                                         op0=ALU.mult), f"ln1_m{h}")
                    edge(v.tensor_tensor(out=sd_all[0:1, hs], in0=m_all[0:1, hs],
                                         in1=m_all[0:1, hs], op=ALU.mult),
                         f"ln1_s{h}")
                    ins = v.scalar_tensor_tensor(out=var_all[0:1, hs],
                                                 in0=ps_b2[0:1, hs],
                                                 scalar=1.0 / D,
                                                 in1=sd_all[0:1, hs],
                                                 op0=ALU.mult, op1=ALU.subtract)
                    S.rbfence(v, ins, VE, f"ln1_var{h}",
                              lambda rd, v=v: v.tensor_copy(out=fb_vr[:], in_=rd),
                              [var_all[0:1, h * 512 + 384:h * 512 + 512],
                               m_all[0:1, h * 512 + 384:h * 512 + 512]])

                @block.scalar
                def _(s, h=h):
                    hs = slice(h * 512, (h + 1) * 512)
                    S.wait(s, f"ln1_var{h}")
                    ins = s.activation(out=sd_all[0:1, hs], in_=var_all[0:1, hs],
                                       func=AF.Ln, bias=eps1[:, 0:1])
                    S.inc(ins, SC, f"ln1_ln{h}")
                    s.wait_ge(S.sems[SC], S.cnt[SC])
                    ins = s.activation(out=rs_all[0:1, hs], in_=sd_all[0:1, hs],
                                       func=AF.Exp, scale=-0.5)
                    S.rbfence(s, ins, SC, f"ln1_rs{h}",
                              lambda rd, s=s: s.copy(out=fb_sr[:], in_=rd),
                              [rs_all[0:1, h * 512 + 384:h * 512 + 512]])

            emit_q(0)
            emit_q(1)
            emit_q(2)
            emit_q(3)

            for ncc in range(NC_):
                @block.tensor
                def _(t, ncc=ncc):
                    cs = slice(ncc * 512, (ncc + 1) * 512)
                    S.wait(t, f"ln1_rs{ncc}", f"ln1_var{ncc}")
                    if ncc < 2:
                        S.wait(t, f"ln1_m{ncc + 2}")
                    if ncc > 0:
                        S.wait(t, f"mb_ev{ncc-1}", f"rb_ev{ncc-1}")
                    b = ncc % 2
                    ins = t.matmul(ps_a2[:, b * 512:(b + 1) * 512], ones_bf1[:, :],
                                   m_all[0:1, cs], start=True, stop=True)
                    S.inc(ins, PE, f"pe_mb{ncc}")
                    ins = t.matmul(ps_a2[:, 1024 + b * 512: 1024 + (b + 1) * 512],
                                   ones_bf1[:, :], rs_all[0:1, cs], start=True, stop=True)
                    S.inc(ins, PE, f"pe_rb{ncc}")

                @block.scalar
                def _(s, ncc=ncc):
                    b = ncc % 2
                    S.wait(s, f"pe_mb{ncc}")
                    ins = s.copy(out=mb_sb[:, b * 512:(b + 1) * 512],
                                 in_=ps_a2[:, b * 512:(b + 1) * 512])
                    S.inc(ins, SC, f"mb_ev{ncc}")
                    S.wait(s, f"pe_rb{ncc}")
                    ins = s.copy(out=rb_sb[:, b * 512:(b + 1) * 512],
                                 in_=ps_a2[:, 1024 + b * 512: 1024 + (b + 1) * 512])
                    S.inc(ins, SC, f"rb_ev{ncc}")

                @block.vector
                def _(v, ncc=ncc):
                    b = ncc % 2
                    S.wait(v, f"mb_ev{ncc}", f"rb_ev{ncc}")
                    if ncc > 0:
                        S.wait(v, f"x1f{ncc-1}_{DC-1}")
                    for dp in range(0, DC, 2):
                        for dc in (dp, dp + 1):
                            ins = v.tensor_tensor(
                                out=x1f[:, dc * 512:(dc + 1) * 512],
                                in0=ybf[:, dc * T + ncc * 512: dc * T + (ncc + 1) * 512],
                                in1=mb_sb[:, b * 512:(b + 1) * 512],
                                op=ALU.subtract)
                        S.fence(v, ins, VE)
                        for dc in (dp, dp + 1):
                            ins = v.tensor_tensor(
                                out=x1bf[:, dc * T + ncc * 512: dc * T + (ncc + 1) * 512],
                                in0=x1f[:, dc * 512:(dc + 1) * 512],
                                in1=rb_sb[:, b * 512:(b + 1) * 512],
                                op=ALU.mult)
                            S.inc(ins, VE, f"x1f{ncc}_{dc}")
                    if ncc == NC_ - 1:
                        S.ev["x1c3_7"] = S.ev[f"x1f{ncc}_{DC-1}"]

                for dc in range(DC):
                    @block.tensor
                    def _(t, ncc=ncc, dc=dc):
                        b = ncc % 2
                        S.wait(t, f"x1f{ncc}_{dc}")
                        if ncc == 0 and dc == 0:
                            S.wait(t, "ld_wrt")
                        if ncc >= 2 and dc == 0:
                            S.wait(t, f"lg_ev{ncc-2}")
                        ins = t.matmul(ps_b2[0:8, b * 512:(b + 1) * 512],
                                       wrt_sb[:, dc * 8:(dc + 1) * 8],
                                       x1bf[:, dc * T + ncc * 512: dc * T + (ncc + 1) * 512],
                                       start=(dc == 0), stop=(dc == DC - 1))
                        S.inc(ins, PE, f"rt{ncc}_{dc}")

                @block.vector
                def _(v, ncc=ncc):
                    b = ncc % 2
                    S.wait(v, f"rt{ncc}_7", "ld_brt")
                    ins = v.tensor_scalar(out=lg_sb[:, ncc * 512:(ncc + 1) * 512],
                                          in0=ps_b2[0:8, b * 512:(b + 1) * 512],
                                          scalar1=brt_sb[:, 0:1], scalar2=None,
                                          op0=ALU.add)
                    S.rbfence(v, ins, VE, f"lg_ev{ncc}",
                              lambda rd, v=v: v.tensor_copy(out=fb_v[0:8, 0:8], in_=rd),
                              [lg_sb[0:8, (ncc + 1) * 512 - 8:(ncc + 1) * 512]])

            if dbg:
                @block.sync
                def _(sy):
                    S.wait(sy, *[f"y_ev{o}" for o in range(DC)])
                    S.dmaq(sy, SL, d_ybf[:, :], ybf[:], None)
                    S.wait(sy, *[f"lg_ev{n}" for n in range(NC_)])
                    S.dmaq(sy, SL, d_lg[:, :], lg_sb[:], None)
                    S.wait(sy, "x1c3_7")
                    S.dmaq(sy, SL, d_x1[:, :], x1bf[:], "dbg_p2")

        pA.close()

        with ExitStack() as p34:
            def p4sb(name, shape, dt=F32):
                return p34.enter_context(nc.sbuf_tensor(name, shape, dt))

            WSLB = 3
            XEB = 3
            wsl = p4sb("wsl", [128, WSLB * DC * D], BF16)
            xe = p4sb("xe", [128, XEB * DC * TQ], BF16)
            wb_all = p4sb("wb_all", [128, NE * T], BF16)
            y2bf = p4sb("y2bf", [128, DC * TQ], BF16)
            sq2 = p4sb("sq2", [128, DC * TQ], BF16)
            m2_row = p4sb("m2_row", [1, TQ], BF16)
            s2_row = p4sb("s2_row", [1, TQ])
            v2_row = p4sb("v2_row", [1, TQ])
            r2_row = p4sb("r2_row", [1, TQ], BF16)
            mb2 = p4sb("mb2", [128, TQ], BF16)
            rb2s = p4sb("rb2s", [128, TQ], BF16)
            pisl = ExitStack()
            def pislsb(name, shape, dt=F32):
                return pisl.enter_context(nc.sbuf_tensor(name, shape, dt))
            lgT = pislsb("lgT", [128, 128])
            wtt = pislsb("wtt", [128, 128])
            m1_16 = pislsb("m1_16", [128, 16])
            m2_16 = pislsb("m2_16", [128, 16])
            den16 = pislsb("den16", [128, 16])
            rec16 = pislsb("rec16", [128, 16])
            eqx = pislsb("eqx", [128, 128])
            mskx = pislsb("mskx", [128, 128])
            e1x = pislsb("e1x", [128, 128])
            expx = pislsb("expx", [128, 128])
            keepx = pislsb("keepx", [128, 128])
            ps_f = [p34.enter_context(nc.psum_tensor(f"ps_f{i}", [128, 512], F32))
                    for i in range(8)]

            lgT3 = lgT[:].rearrange("p (i e) -> p i e", i=16)
            mskx3 = mskx[:].rearrange("p (i e) -> p i e", i=16)
            expx3 = expx[:].rearrange("p (i e) -> p i e", i=16)
            m1b = bass.AP(m1_16, 0, [[16, 128], [1, 16], [0, 8]])
            m2b = bass.AP(m2_16, 0, [[16, 128], [1, 16], [0, 8]])
            recb = bass.AP(rec16, 0, [[16, 128], [1, 16], [0, 8]])

            for i in range(16):
                bk = 4 + (i % 2)

                @block.tensor
                def _(t, i=i, bk=bk):
                    S.wait(t, f"lg_ev{i//4}")
                    if i == 0:
                        S.wait(t, "mb_ev3", "rb_ev3", "lg_ev3", "x1c3_7")
                    if i >= 2:
                        S.wait(t, f"lgT_ev{i-2}")
                    ins = t.transpose(ps_f[bk][:, 0:8],
                                      lg_sb[0:8, i * 128:(i + 1) * 128], id_f8[:, :])
                    S.inc(ins, PE, f"pe_tp{i}")

                @block.scalar
                def _(s, i=i, bk=bk):
                    S.wait(s, f"pe_tp{i}")
                    if i == 0:
                        S.wait(s, "x1c3_7")
                        if dbg:
                            S.wait(s, "dbg_p2")
                    ins = s.copy(out=lgT[:, i * 8:(i + 1) * 8], in_=ps_f[bk][:, 0:8])
                    if i in (7, 15):
                        S.rbfence(s, ins, SC, f"lgT_ev{i}",
                                  lambda rd, s=s, i=i: s.copy(out=fb_s[:, 8:16], in_=rd),
                                  [lgT[:, i * 8:(i + 1) * 8]])
                    else:
                        S.inc(ins, SC, f"lgT_ev{i}")

            for h in range(2):
                hs = slice(h * 64, (h + 1) * 64)
                h8 = slice(h * 8, (h + 1) * 8)

                @block.vector
                def _(v, h=h, hs=hs, h8=h8):
                    lgT3h = lgT[:, hs].rearrange("p (i e) -> p i e", i=8)
                    mskx3h = mskx[:, hs].rearrange("p (i e) -> p i e", i=8)
                    m1bh = bass.AP(m1_16, h * 8, [[16, 128], [1, 8], [0, 8]])
                    S.wait(v, f"lgT_ev{h * 8 + 7}")
                    if h == 0:
                        S.wait(v, "x1c3_7")
                        if dbg:
                            S.wait(v, "dbg_p2")

                    def edge(ins, tag, v=v):
                        S.inc(ins, VE, tag)
                        S.wait(v, tag)
                    edge(v.reduce_max(out=m1_16[:, h8], in_=lgT3h, axis=AX.X),
                         f"isl_m1{h}")
                    ins = v.tensor_tensor(
                        out=e1x[:, hs].rearrange("p (i e) -> p i e", i=8),
                        in0=lgT3h, in1=m1bh, op=ALU.subtract)
                    S.rbfence(v, ins, VE, f"isl_e1{h}",
                              lambda rd, v=v: v.tensor_copy(out=fb_v[:, 0:8], in_=rd),
                              [e1x[:, h * 64 + 56:h * 64 + 64]])
                    edge(v.tensor_tensor(
                        out=eqx[:, hs].rearrange("p (i e) -> p i e", i=8),
                        in0=lgT3h, in1=m1bh, op=ALU.is_equal), f"isl_eq{h}")
                    edge(v.scalar_tensor_tensor(out=mskx[:, hs], in0=eqx[:, hs],
                                                scalar=-1e30, in1=lgT[:, hs],
                                                op0=ALU.mult, op1=ALU.add),
                         f"isl_msk{h}")
                    edge(v.reduce_max(out=m2_16[:, h8], in_=mskx3h, axis=AX.X),
                         f"isl_m2{h}")
                    edge(v.tensor_tensor(
                        out=keepx[:, hs].rearrange("p (i e) -> p i e", i=8),
                        in0=lgT3h,
                        in1=bass.AP(m2_16, h * 8, [[16, 128], [1, 8], [0, 8]]),
                        op=ALU.is_ge), f"isl_keep{h}")

            for h in range(2):
                hs = slice(h * 64, (h + 1) * 64)

                @block.scalar
                def _(s, h=h, hs=hs):
                    S.wait(s, f"isl_e1{h}")
                    ins = s.activation(out=expx[:, hs], in_=e1x[:, hs], func=AF.Exp)
                    S.rbfence(s, ins, SC, f"isl_exp{h}",
                              lambda rd, s=s: s.copy(out=fb_s[:, 8:16], in_=rd),
                              [expx[:, h * 64 + 56:h * 64 + 64]])

            for h in range(2):
                hs = slice(h * 64, (h + 1) * 64)
                h8 = slice(h * 8, (h + 1) * 8)

                @block.vector
                def _(v, h=h, hs=hs, h8=h8):
                    expx3h = expx[:, hs].rearrange("p (i e) -> p i e", i=8)
                    recbh = bass.AP(rec16, h * 8, [[16, 128], [1, 8], [0, 8]])
                    S.wait(v, f"isl_exp{h}")

                    def edge(ins, tag, v=v):
                        S.inc(ins, VE, tag)
                        S.wait(v, tag)
                    edge(v.reduce_sum(out=den16[:, h8], in_=expx3h, axis=AX.X),
                         f"isl_den{h}")
                    edge(v.reciprocal(out=rec16[:, h8], in_=den16[:, h8]),
                         f"isl_rec{h}")
                    edge(v.tensor_tensor(
                        out=e1x[:, hs].rearrange("p (i e) -> p i e", i=8),
                        in0=expx3h, in1=recbh, op=ALU.mult), f"isl_t2{h}")
                    ins = v.tensor_tensor(out=wtt[:, hs], in0=e1x[:, hs],
                                          in1=keepx[:, hs], op=ALU.mult)
                    S.rbfence(v, ins, VE, f"isl_w{h}",
                              lambda rd, v=v: v.tensor_copy(out=fb_v[:, 0:8], in_=rd),
                              [wtt[:, h * 64 + 56:h * 64 + 64]])

            for i in range(16):
                bk = 6 + (i % 2)

                @block.tensor
                def _(t, i=i, bk=bk):
                    S.wait(t, f"isl_w{i // 8}")
                    if i >= 2:
                        S.wait(t, f"wfm_ev{i-2}")
                    ins = t.transpose(ps_f[bk][0:8, 0:128],
                                      wtt[:, i * 8:(i + 1) * 8], id_f[:, :])
                    S.inc(ins, PE, f"pe_tw{i}")

                @block.scalar
                def _(s, i=i, bk=bk):
                    S.wait(s, f"pe_tw{i}")
                    ins = s.copy(out=wfm[0:8, i * 128:(i + 1) * 128],
                                 in_=ps_f[bk][0:8, 0:128])
                    if i % 4 == 3:
                        S.rbfence(s, ins, SC, f"wfm_ev{i}",
                                  lambda rd, s=s: s.copy(out=fb_s[0:8, 0:16], in_=rd),
                                  [wfm[0:8, (i + 1) * 128 - 16:(i + 1) * 128]])
                    else:
                        S.inc(ins, SC, f"wfm_ev{i}")

            for qq in range(4):
                for e in range(NE):
                    bk2 = (qq * NE + e) % 4

                    @block.tensor
                    def _(t, qq=qq, e=e, bk2=bk2):
                        S.wait(t, f"wfm_ev{qq * 4 + 3}")
                        gidx = qq * NE + e
                        if gidx >= 4:
                            S.wait(t, f"wbev{(gidx-4)//NE}_{(gidx-4)%NE}")
                        ins = t.matmul(ps_f[bk2][:, :],
                                       sel_bf[:, e * 128:(e + 1) * 128],
                                       wfm[0:8, qq * TQ:(qq + 1) * TQ],
                                       start=True, stop=True)
                        S.inc(ins, PE, f"pe_wb{qq}_{e}")

                    @block.vector
                    def _(v, qq=qq, e=e, bk2=bk2):
                        S.wait(v, f"pe_wb{qq}_{e}")
                        ins = v.tensor_copy(
                            out=wb_all[:, e * T + qq * TQ: e * T + (qq + 1) * TQ],
                            in_=ps_f[bk2][:, :])
                        S.inc(ins, VE, f"wbev{qq}_{e}")

            if dbg:
                @block.sync
                def _(sy):
                    S.wait(sy, *[f"wfm_ev{i}" for i in range(16)])
                    S.dmaq(sy, SL, d_wfm[:, :], wfm[:], "dbg_p3")

            pisl.close()
            outr = p4sb("outr", [128, 4 * TQ])
            otmp = p4sb("otmp", [128, 2 * TQ], BF16)

            OCROT = [4, 5, 6, 7, 2, 3, 0, 1]

            def emit_ln2_tail_a(q):
                for oc in range(NE):
                    @block.tensor
                    def _(t, q=q, oc=oc):
                        S.wait(t, f"sq2{q}_{oc}")
                        if oc == 0:
                            S.wait(t, f"y2_ev{q}_0", f"y2_ev{q}_1")
                        t.matmul(ps_f[0][0:1, :], ones_bf[:, :],
                                 y2bf[:, oc * TQ:(oc + 1) * TQ],
                                 start=(oc == 0), stop=(oc == NE - 1))
                        ins = t.matmul(ps_f[1][0:1, :], ones_bf[:, :],
                                       sq2[:, oc * TQ:(oc + 1) * TQ],
                                       start=(oc == 0), stop=(oc == NE - 1))
                        S.inc(ins, PE, f"st2{q}_{oc}")

                @block.vector
                def _(v, q=q):
                    S.wait(v, f"st2{q}_{NE-1}")

                    def edge(ins, tag, v=v, q=q):
                        S.inc(ins, VE, f"{tag}e{q}")
                        S.wait(v, f"{tag}e{q}")
                    edge(v.tensor_scalar(out=m2_row[:], in0=ps_f[0][0:1, :],
                                         scalar1=1.0 / D, scalar2=None,
                                         op0=ALU.mult), "m2")
                    edge(v.tensor_tensor(out=s2_row[:], in0=m2_row[:], in1=m2_row[:],
                                         op=ALU.mult), "s2")
                    ins = v.scalar_tensor_tensor(out=v2_row[:], in0=ps_f[1][0:1, :],
                                                 scalar=1.0 / D, in1=s2_row[:],
                                                 op0=ALU.mult, op1=ALU.subtract)
                    S.rbfence(v, ins, VE, f"st_ev{q}",
                              lambda rd, v=v: v.tensor_copy(out=fb_vr[:], in_=rd),
                              [v2_row[0:1, 384:512], m2_row[0:1, 384:512]])

                @block.scalar
                def _(s, q=q):
                    S.wait(s, f"st_ev{q}")
                    ins = s.activation(out=s2_row[:], in_=v2_row[:], func=AF.Ln,
                                       bias=eps1[:, 0:1])
                    S.inc(ins, SC, f"ln2l{q}")
                    s.wait_ge(S.sems[SC], S.cnt[SC])
                    ins = s.activation(out=r2_row[:], in_=s2_row[:], func=AF.Exp,
                                       scale=-0.5)
                    S.rbfence(s, ins, SC, f"rs2_ev{q}",
                              lambda rd, s=s: s.copy(out=fb_sr[:], in_=rd),
                              [r2_row[0:1, 384:512]])

            def emit_ln2_bcast(q):
                @block.tensor
                def _(t, q=q):
                    S.wait(t, f"rs2_ev{q}", f"st_ev{q}")
                    ins = t.matmul(ps_f[2][:, :], ones_bf1[:, :], m2_row[0:1, :],
                                   start=True, stop=True)
                    S.inc(ins, PE, f"pe_m2{q}")
                    ins = t.matmul(ps_f[3][:, :], ones_bf1[:, :], r2_row[0:1, :],
                                   start=True, stop=True)
                    S.inc(ins, PE, f"pe_r2{q}")

                @block.scalar
                def _(s, q=q):
                    S.wait(s, f"pe_m2{q}")
                    if q > 0:
                        S.wait(s, f"o_ev{q-1}_{NE-1}")
                    ins = s.copy(out=mb2[:], in_=ps_f[2][:, :])
                    S.inc(ins, SC, f"mb2_ev{q}")
                    S.wait(s, f"pe_r2{q}")
                    ins = s.copy(out=rb2s[:], in_=ps_f[3][:, :])
                    S.inc(ins, SC, f"rb2_ev{q}")

            def emit_ln2_apply(q):
                qs = slice(q * TQ, (q + 1) * TQ)
                for oc in range(NE):
                    gi = q * NE + oc
                    ti = gi % 2
                    ri = gi % 4

                    @block.vector
                    def _(v, q=q, oc=oc, ti=ti, gi=gi):
                        S.wait(v, f"mb2_ev{q}", f"rb2_ev{q}")
                        if gi >= 2:
                            S.wait(v, f"o_ev{(gi-2)//NE}_{(gi-2)%NE}")
                        ins = v.tensor_tensor(out=otmp[:, ti * TQ:(ti + 1) * TQ],
                                              in0=y2bf[:, oc * TQ:(oc + 1) * TQ],
                                              in1=mb2[:], op=ALU.subtract)
                        S.inc(ins, VE, f"o1_{q}_{oc}")

                    @block.vector
                    def _(v, q=q, oc=oc, ti=ti, ri=ri, gi=gi):
                        S.wait(v, f"o1_{q}_{oc}")
                        if gi >= 4:
                            S.wait(v, f"od{(gi-4)//NE}_{(gi-4)%NE}")
                        ins = v.tensor_tensor(out=outr[:, ri * TQ:(ri + 1) * TQ],
                                              in0=otmp[:, ti * TQ:(ti + 1) * TQ],
                                              in1=rb2s[:], op=ALU.mult)
                        S.inc(ins, VE, f"o_ev{q}_{oc}")

                    if oc % 2 == 0:
                        @block.scalar
                        def _(sc2, q=q, oc=oc, ri=ri):
                            S.wait(sc2, f"o_ev{q}_{oc}")
                            S.dmaq(sc2, OU, out[oc * 128:(oc + 1) * 128, qs],
                                   outr[:, ri * TQ:(ri + 1) * TQ], f"od{q}_{oc}")
                    else:
                        @block.sync
                        def _(sy, q=q, oc=oc, ri=ri):
                            S.wait(sy, f"o_ev{q}_{oc}")
                            S.dmaq(sy, SL, out[oc * 128:(oc + 1) * 128, qs],
                                   outr[:, ri * TQ:(ri + 1) * TQ], f"od{q}_{oc}")

            for q in range(4):
                qs = slice(q * TQ, (q + 1) * TQ)

                def emit_wsl(q, e):
                    eb = (q * NE + e) % WSLB

                    @block.sync
                    def _(sy, q=q, e=e, eb=eb):
                        if q == 0:
                            S.wait(sy, f"wec{e}")
                        if q == 0 and e < WSLB:
                            S.wait(sy, "x1c3_7", "lg_ev3", "rt3_7")
                            if dbg:
                                S.wait(sy, "dbg_p2")
                        if q * NE + e >= WSLB:
                            pe_i = (q * NE + e) - WSLB
                            S.wait(sy, f"ffn{pe_i//NE}_{pe_i%NE}")
                        S.dmaq(sy, SL,
                               wsl[:, eb * DC * D:(eb + 1) * DC * D].rearrange(
                                   "p (dc o) -> p dc o", dc=DC),
                               wec.ap()[e, :, :].rearrange("(dc p) o -> p dc o", p=128),
                               f"wsl{q}_{e}")

                def emit_xe(q, e):
                    xb = (q * NE + e) % XEB

                    @block.vector
                    def _(v, q=q, e=e, xb=xb):
                        S.wait(v, f"wbev{q}_{e}")
                        if q * NE + e >= XEB:
                            pe_i = (q * NE + e) - XEB
                            S.wait(v, f"ffn{pe_i//NE}_{pe_i%NE}")
                        for dc in range(DC):
                            ins = v.tensor_tensor(
                                out=xe[:, (xb * DC + dc) * TQ:(xb * DC + dc + 1) * TQ],
                                in0=x1bf[:, dc * T + q * TQ: dc * T + q * TQ + TQ],
                                in1=wb_all[:, e * T + q * TQ: e * T + (q + 1) * TQ],
                                op=ALU.mult)
                        S.inc(ins, VE, f"xe{q}_{e}")

                def ffn_mms(t, q, e, eb, xb, ocs, ocw=None):
                    ins = None
                    for oc in ocs:
                        if ocw is not None:
                            S.wait(t, *ocw(oc))
                        if e == 0 and q == 0 and oc in (0, 1, 2, 3):
                            S.wait(t, f"wbev{3}_{oc+4}")
                        for dc in range(DC):
                            lhsT = wsl[:, (eb * DC + dc) * D + oc * 128:
                                       (eb * DC + dc) * D + (oc + 1) * 128]
                            ins = t.matmul(ps_f[oc][:, :], lhsT,
                                           xe[:, (xb * DC + dc) * TQ:(xb * DC + dc + 1) * TQ],
                                           start=(e == 0 and dc == 0), stop=False)
                    return ins

                if q == 0:
                    for e in range(NE):
                        emit_wsl(q, e)
                        emit_xe(q, e)

                        @block.tensor
                        def _(t, q=q, e=e):
                            eb = (q * NE + e) % WSLB
                            xb = (q * NE + e) % XEB
                            S.wait(t, f"xe{q}_{e}", f"wsl{q}_{e}")
                            if e == 0:
                                S.wait(t, "lgT_ev15", "wfm_ev15")
                            ins = ffn_mms(t, q, e, eb, xb, OCROT)
                            S.inc(ins, PE, f"ffn{q}_{e}")
                else:
                    NA = 3
                    for e in range(NA):
                        if e >= 2:
                            emit_wsl(q, e)
                            emit_xe(q, e)

                        @block.tensor
                        def _(t, q=q, e=e):
                            eb = (q * NE + e) % WSLB
                            xb = (q * NE + e) % XEB
                            S.wait(t, f"xe{q}_{e}", f"wsl{q}_{e}")
                            ocw = None
                            if e == 0:
                                ocw = lambda oc, q=q: [f"y2_ev{q-1}_{oc}"]
                            ffn_mms(t, q, e, eb, xb, [4, 5, 6, 7], ocw=ocw)

                    emit_ln2_tail_a(q - 1)

                    @block.tensor
                    def _(t, q=q, NA=NA):
                        S.wait(t, f"st_ev{q-1}")
                        eb = (q * NE) % WSLB
                        xb = (q * NE) % XEB
                        ffn_mms(t, q, 0, eb, xb, [0, 1])

                    emit_ln2_bcast(q - 1)

                    @block.tensor
                    def _(t, q=q, NA=NA):
                        S.wait(t, f"mb2_ev{q-1}", f"rb2_ev{q-1}")
                        for e in range(NA):
                            eb = (q * NE + e) % WSLB
                            xb = (q * NE + e) % XEB
                            if e > 0:
                                ffn_mms(t, q, e, eb, xb, [0, 1])
                            ins = ffn_mms(t, q, e, eb, xb, [2, 3])
                            S.inc(ins, PE, f"ffn{q}_{e}")

                    def expert_sec(e):
                        @block.tensor
                        def _(t, q=q, e=e):
                            eb = (q * NE + e) % WSLB
                            xb = (q * NE + e) % XEB
                            S.wait(t, f"xe{q}_{e}", f"wsl{q}_{e}")
                            ins = ffn_mms(t, q, e, eb, xb, OCROT)
                            S.inc(ins, PE, f"ffn{q}_{e}")

                    for e in range(NA, NA + 3):
                        emit_wsl(q, e)
                        emit_xe(q, e)
                    expert_sec(NA)
                    expert_sec(NA + 1)
                    for e in range(NA + 3, NE):
                        emit_wsl(q, e)
                        emit_xe(q, e)

                    emit_ln2_apply(q - 1)

                    for e in range(NA + 2, NE):
                        expert_sec(e)

                if q < 3:
                    for e in range(2):
                        emit_wsl(q + 1, e)
                        emit_xe(q + 1, e)

                for oc in ((4, 5, 6, 7, 0, 1, 2, 3) if q < 3 else
                           (0, 1, 2, 3, 4, 5, 6, 7)):
                    @block.tensor
                    def _(t, q=q, oc=oc):
                        if q == 0 and oc == 0:
                            S.wait(t, "ld_bex")
                        ins = t.matmul(ps_f[oc][:, :], bex_bf[:, oc * 128:(oc + 1) * 128],
                                       wfm[0:8, qs], start=False, stop=True)
                        S.inc(ins, PE, f"bia{q}_{oc}")

                    @block.vector
                    def _(v, q=q, oc=oc):
                        S.wait(v, f"bia{q}_{oc}")
                        if dbg and q == 1 and oc == 0:
                            S.wait(v, "dbg_p4")
                        if q > 0:
                            S.wait(v, f"st2{q-1}_{oc}")
                        ins = v.tensor_tensor(out=y2bf[:, oc * TQ:(oc + 1) * TQ],
                                              in0=ps_f[oc][:, :],
                                              in1=x1bf[:, oc * T + q * TQ: oc * T + q * TQ + TQ],
                                              op=ALU.add)
                        S.inc(ins, VE, f"y2_ev{q}_{oc}")

                    @block.vector
                    def _(v2e, q=q, oc=oc):
                        S.wait(v2e, f"y2_ev{q}_{oc}")
                        if q > 0:
                            S.wait(v2e, f"st2{q-1}_{oc}")
                        ins = v2e.tensor_tensor(out=sq2[:, oc * TQ:(oc + 1) * TQ],
                                                in0=y2bf[:, oc * TQ:(oc + 1) * TQ],
                                                in1=y2bf[:, oc * TQ:(oc + 1) * TQ],
                                                op=ALU.mult)
                        S.inc(ins, VE, f"sq2{q}_{oc}")

                if dbg and q == 0:
                    @block.sync
                    def _(sy):
                        S.wait(sy, f"wbev0_{NE-1}")
                        S.dmaq(sy, SL, d_wb[:, :],
                               wb_all[:].rearrange("p (e t) -> p e t", e=NE)[:, :, 0:TQ],
                               None)
                        S.wait(sy, *[f"y2_ev0_{o}" for o in range(NE)])
                        S.dmaq(sy, SL, d_y2[:, :], y2bf[:], "dbg_p4")

            emit_ln2_tail_a(3)
            emit_ln2_bcast(3)
            emit_ln2_apply(3)

            @block.sync
            def _(sy):
                for qn in (OU, OU + "1", OU + "2", SL, SL + "1", SL + "2"):
                    if S.val(qn) > 0:
                        sy.wait_ge(S.sems[qn], S.val(qn))

    return nc


_NC_CACHE = None


def kernel(L0_attn_x, L0_attn_Wk, L0_attn_Wv, L0_attn_Wr, L0_attn_Wo,
           L0_attn_time_decay, L0_attn_time_first,
           L0_ffn_W_experts, L0_ffn_b_experts, L0_ffn_W_router, L0_ffn_b_router):
    global _NC_CACHE
    B = L0_attn_x.shape[0]
    assert L0_attn_x.shape == (8, T, D)

    wkT = np.ascontiguousarray(np.asarray(L0_attn_Wk, np.float32).T)
    wvT = np.ascontiguousarray(np.asarray(L0_attn_Wv, np.float32).T)
    wrT = np.ascontiguousarray(np.asarray(L0_attn_Wr, np.float32).T)
    woT = np.ascontiguousarray(np.asarray(L0_attn_Wo, np.float32).T)
    weT = np.ascontiguousarray(np.asarray(L0_ffn_W_experts, np.float32).transpose(0, 2, 1))
    wrtH = np.ascontiguousarray(
        np.asarray(L0_ffn_W_router, np.float32).T.reshape(DC, 128, NE).transpose(1, 0, 2))
    brtH = np.ascontiguousarray(np.asarray(L0_ffn_b_router, np.float32).reshape(NE, 1))
    bexH = np.ascontiguousarray(np.asarray(L0_ffn_b_experts, np.float32))
    tdH = np.ascontiguousarray(np.asarray(L0_attn_time_decay, np.float32).reshape(DC, 128).T)
    tfH = np.ascontiguousarray(np.asarray(L0_attn_time_first, np.float32).reshape(DC, 128).T)

    x = np.asarray(L0_attn_x, np.float32)
    in_maps = []
    for b in range(B):
        in_maps.append({
            "xT": np.ascontiguousarray(x[b].T),
            "wk": wkT, "wv": wvT, "wr": wrT, "wo": woT, "we": weT,
            "wrt": wrtH, "brt": brtH, "bex": bexH, "td": tdH, "tf": tfH,
        })

    if _NC_CACHE is None:
        _NC_CACHE = build_nc()
    res = run_bass_kernel_spmd(_NC_CACHE, in_maps, core_ids=list(range(8)))
    outs = np.stack([np.asarray(r["out"], np.float32).T for r in res.results], axis=0)
    return outs


# version 50
# speedup vs baseline: 1.0060x; 1.0041x over previous
import numpy as np
import concourse.bass as bass
import concourse.mybir as mybir
from concourse.bass_utils import run_bass_kernel_spmd
from contextlib import ExitStack

F32 = mybir.dt.float32
BF16 = mybir.dt.bfloat16
I32 = mybir.dt.int32
AF = mybir.ActivationFunctionType
ALU = mybir.AluOpType
AX = mybir.AxisListType

T = 2048
NC_ = 4
TQ = 512
D = 1024
DC = 8
NE = 8
EPS = 1e-5


class Sched:

    def __init__(self):
        self.sems = {}
        self.cnt = {}
        self.ev = {}
        self.qrot = {}

    def add(self, name, handle):
        self.sems[name] = handle
        self.cnt[name] = 0

    def inc(self, ins, sem, event=None, k=1):
        self.cnt[sem] += k
        if event is not None:
            self.ev[event] = (self.sems[sem], self.cnt[sem])
        return ins.then_inc(self.sems[sem], k)

    def val(self, sem):
        return self.cnt[sem]

    def fence(self, eng, ins, sem, event=None):
        self.cnt[sem] += 1
        ins.then_inc(self.sems[sem], 1)
        eng.wait_ge(self.sems[sem], self.cnt[sem])
        if event is not None:
            self.ev[event] = (self.sems[sem], self.cnt[sem])

    def rbfence(self, eng, ins, sem, event, copyfn, reads):
        self.cnt[sem] += 1
        ins.then_inc(self.sems[sem], 1)
        eng.wait_ge(self.sems[sem], self.cnt[sem])
        last = None
        for rd in reads:
            if last is not None:
                self.cnt[sem] += 1
                last.then_inc(self.sems[sem], 1)
                eng.wait_ge(self.sems[sem], self.cnt[sem])
            last = copyfn(rd)
        self.cnt[sem] += 1
        last.then_inc(self.sems[sem], 1)
        self.ev[event] = (self.sems[sem], self.cnt[sem])

    def dmaq(self, eng, sem, out, in_, event=None):
        use = sem
        if sem + "1" in self.sems:
            r = self.qrot.get(sem, 0)
            self.qrot[sem] = r + 1
            use = (sem, sem + "1", sem + "2")[r % 3]
        if self.cnt[use] > 0:
            eng.wait_ge(self.sems[use], self.cnt[use])
        ins = eng.dma_start(out=out, in_=in_)
        return self.inc(ins, use, event, 16)

    def wait(self, eng, *events):
        need = {}
        for e in events:
            if e is None:
                continue
            h, v = self.ev[e]
            if need.get(id(h), (None, -1))[1] < v:
                need[id(h)] = (h, v)
        for h, v in need.values():
            eng.wait_ge(h, v)


def build_nc(dbg=False):
    nc = bass.Bass("TRN2", target_bir_lowering=False)

    xT = nc.dram_tensor("xT", [D, T], F32, kind="ExternalInput").ap()
    wk = nc.dram_tensor("wk", [D, D], F32, kind="ExternalInput").ap()
    wv = nc.dram_tensor("wv", [D, D], F32, kind="ExternalInput").ap()
    wr = nc.dram_tensor("wr", [D, D], F32, kind="ExternalInput").ap()
    wo = nc.dram_tensor("wo", [D, D], F32, kind="ExternalInput").ap()
    we = nc.dram_tensor("we", [NE, D, D], F32, kind="ExternalInput").ap()
    wrt = nc.dram_tensor("wrt", [128, DC, NE], F32, kind="ExternalInput").ap()
    brt = nc.dram_tensor("brt", [NE, 1], F32, kind="ExternalInput").ap()
    bex = nc.dram_tensor("bex", [NE, D], F32, kind="ExternalInput").ap()
    td = nc.dram_tensor("td", [128, DC], F32, kind="ExternalInput").ap()
    tf = nc.dram_tensor("tf", [128, DC], F32, kind="ExternalInput").ap()
    out = nc.dram_tensor("out", [D, T], F32, kind="ExternalOutput").ap()
    wec = nc.dram_tensor("wec", [NE, D, D], BF16)
    if dbg:
        d_dcol = nc.dram_tensor("d_dcol", [128, DC], F32, kind="ExternalOutput").ap()
        d_eu = nc.dram_tensor("d_eu", [128, DC], F32, kind="ExternalOutput").ap()
        d_attn = nc.dram_tensor("d_attn", [128, DC * T], BF16, kind="ExternalOutput").ap()
        d_ybf = nc.dram_tensor("d_ybf", [128, DC * T], BF16, kind="ExternalOutput").ap()
        d_lg = nc.dram_tensor("d_lg", [8, T], F32, kind="ExternalOutput").ap()
        d_wfm = nc.dram_tensor("d_wfm", [8, T], BF16, kind="ExternalOutput").ap()
        d_x1 = nc.dram_tensor("d_x1", [128, DC * T], BF16, kind="ExternalOutput").ap()
        d_y2 = nc.dram_tensor("d_y2", [128, DC * TQ], BF16, kind="ExternalOutput").ap()
        d_wb = nc.dram_tensor("d_wb", [128, NE * TQ], BF16, kind="ExternalOutput").ap()
        d_kf = nc.dram_tensor("d_kf", [128, T], F32, kind="ExternalOutput").ap()
        d_isl = nc.dram_tensor("d_isl", [128, 64], F32, kind="ExternalOutput").ap()
        d_vf = nc.dram_tensor("d_vf", [128, T], BF16, kind="ExternalOutput").ap()
        d_rf = nc.dram_tensor("d_rf", [128, T], BF16, kind="ExternalOutput").ap()
        d_xbf = nc.dram_tensor("d_xbf", [128, DC * T], BF16, kind="ExternalOutput").ap()
        d_wkb = nc.dram_tensor("d_wkb", [128, DC * D], BF16, kind="ExternalOutput").ap()
        d_ss = nc.dram_tensor("d_ss", [128, T], F32, kind="ExternalOutput").ap()

    S = Sched()

    with ExitStack() as outer:
        def osb(name, shape, dt=F32):
            return outer.enter_context(nc.sbuf_tensor(name, shape, dt))

        x1bf = osb("x1bf", [128, DC * T], BF16)
        lg_sb = osb("lg_sb", [8, T])
        id_f = osb("id_f", [128, 128])
        td_sb = osb("td_sb", [128, DC])
        tf_sb = osb("tf_sb", [128, DC])
        dcol8 = osb("dcol8", [128, DC])
        eucol8 = osb("eucol8", [128, DC])
        lam8 = osb("lam8", [128, DC])
        neglam8 = osb("neglam8", [128, DC])
        fb_v = osb("fb_v", [128, 16])
        fb_s = osb("fb_s", [128, 16])
        fb_vr = osb("fb_vr", [1, 128])
        fb_sr = osb("fb_sr", [1, 128])
        ones_bf = osb("ones_bf", [128, 1], BF16)
        eps1 = osb("eps1", [1, 1])
        ones_f1 = osb("ones_f1", [1, 128])
        ones_bf1 = osb("ones_bf1", [1, 128], BF16)
        iota_c = osb("iota_c", [128, 128], I32)
        iota_r8 = osb("iota_r8", [8, 128], I32)
        id_bf = osb("id_bf", [128, 128], BF16)
        id_f8 = osb("id_f8", [8, 8])
        sel_bf = osb("sel_bf", [8, NE * 128], BF16)
        wfm = osb("wfm", [8, T], BF16)
        brt_sb = osb("brt_sb", [NE, 1])
        bex_bf = osb("bex_bf", [NE, D], BF16)
        wrt_sb = osb("wrt_sb", [128, DC * NE], BF16)

        for nm in ["gl", "sl", "pe", "ve", "sc", "po", "ou",
                   "gl1", "gl2", "sl1", "sl2", "ou1", "ou2"]:
            S.add(nm, outer.enter_context(nc.semaphore("sem_" + nm)))
        GL, SL, PE, VE, SC, PO, OU = "gl", "sl", "pe", "ve", "sc", "po", "ou"

        block = outer.enter_context(nc.Block())
        pA = outer.enter_context(ExitStack())
        attn_in = pA.enter_context(nc.sbuf_tensor("attn_in", [128, DC * T], BF16))
        wkb = pA.enter_context(nc.sbuf_tensor("wkb", [128, DC * D], BF16))

        @block.gpsimd
        def _(g):
            pass

        @block.gpsimd
        def _(g):
            g.iota(iota_c[:], [[1, 128]], base=0, channel_multiplier=-1)
            S.inc(g.iota(iota_r8[:], [[0, 128]], base=0, channel_multiplier=1),
                  PO, "iota_done")

        @block.vector
        def _(v):
            v.memset(ones_bf[:], 1.0)
            v.memset(ones_bf1[:], 1.0)
            v.memset(eps1[:], EPS)
            v.memset(ones_f1[:], 1.0)
            S.wait(v, "iota_done")
            v.tensor_scalar(out=id_bf[:], in0=iota_c[:], scalar1=0.0,
                            scalar2=None, op0=ALU.is_equal)
            v.tensor_scalar(out=id_f8[:], in0=iota_c[0:8, 0:8], scalar1=0.0,
                            scalar2=None, op0=ALU.is_equal)
            v.tensor_scalar(out=id_f[:], in0=iota_c[:], scalar1=0.0,
                            scalar2=None, op0=ALU.is_equal)
            for e in range(NE):
                ins = v.tensor_scalar(out=sel_bf[:, e * 128:(e + 1) * 128],
                                      in0=iota_r8[:], scalar1=float(e),
                                      scalar2=None, op0=ALU.is_equal)
            S.inc(ins, VE, "consts_done")

        if dbg:
            @block.sync
            def _(sy):
                S.wait(sy, "decay_done")
                S.dmaq(sy, SL, d_dcol[:, :], dcol8[:], None)
                S.dmaq(sy, SL, d_eu[:, :], eucol8[:], None)

        with ExitStack() as p1:
            def p1sb(name, shape, dt=F32):
                return p1.enter_context(nc.sbuf_tensor(name, shape, dt))

            xbf = p1sb("xbf", [128, DC * T], BF16)
            iota1p = p1sb("iota1p", [128, 1024])
            wvb = p1sb("wvb", [128, DC * 512], BF16)
            wrb = p1sb("wrb", [128, DC * 512], BF16)
            k_f = p1sb("k_f", [128, T])
            v_f = p1sb("v_f", [128, T], BF16)
            r_f = p1sb("r_f", [128, T], BF16)
            kv_f = p1sb("kv_f", [128, T])
            ss = p1sb("ss", [128, T + 4])
            ssc = p1sb("ssc", [128, 1024], BF16)
            posd = p1sb("posd", [128, 1024], BF16)
            ps_a = p1.enter_context(nc.psum_tensor("ps_a", [128, T], F32))
            ps_b = p1.enter_context(nc.psum_tensor("ps_b", [128, T], F32))

            @block.gpsimd
            def _(g):
                def wgroup(dst, src, e0, e1, nm, ns=DC, pre=None):
                    if pre is not None:
                        S.wait(g, pre)
                    S.dmaq(g, GL,
                           bass.AP(dst, (e0 % ns) * 128,
                                   [[DC * ns * 128, 128], [ns * 128, DC],
                                    [1, (e1 - e0) * 128]]),
                           src[:, e0 * 128:e1 * 128].rearrange(
                               "(dc p) o -> p dc o", p=128), f"{nm}{e0}")
                    for e in range(e0 + 1, e1):
                        S.ev[f"{nm}{e}"] = S.ev[f"{nm}{e0}"]
                wgroup(wkb, wk, 0, 1, "ld_wk")
                S.dmaq(g, GL, bass.AP(xbf, 512, [[DC * T, 128], [T, DC], [1, 512]]),
                       xT[:, 512:1024].rearrange(
                           "(dc p) t -> p dc t", p=128), "ld_xn1")
                S.inc(g.iota(iota1p[:], [[1, 1024]], base=1, channel_multiplier=0,
                             allow_small_or_imprecise_dtypes=True), PO, "iota1p_done")
                wgroup(wvb, wv, 0, 1, "ld_wv", ns=4)
                wgroup(wrb, wr, 0, 1, "ld_wr", ns=4)
                S.dmaq(g, GL, bass.AP(xbf, 1536, [[DC * T, 128], [T, DC], [1, 512]]),
                       xT[:, 1536:2048].rearrange(
                           "(dc p) t -> p dc t", p=128), "ld_xn3")
                wgroup(wkb, wk, 1, 4, "ld_wk")
                wgroup(wvb, wv, 1, 4, "ld_wv", ns=4)
                wgroup(wrb, wr, 1, 4, "ld_wr", ns=4)
                wgroup(wkb, wk, 4, 8, "ld_wk")

            prev = None
            for ncc in (0, 2):
                @block.sync
                def _(sy, ncc=ncc, prev=prev):
                    for half, stg in ((0, k_f), (1, kv_f)):
                        if prev is not None:
                            S.wait(sy, f"c{prev}_{half}")
                        S.dmaq(sy, SL,
                               stg[:].rearrange("p (dc t) -> p dc t", dc=4),
                               xT[half * 512:(half + 1) * 512,
                                  ncc * 512:(ncc + 1) * 512]
                               .rearrange("(dc p) t -> p dc t", p=128),
                               f"xs{ncc}_{half}")
                    if ncc == 0:
                        S.dmaq(sy, SL, td_sb[:], td[:, :], "ld_td")
                        S.dmaq(sy, SL, tf_sb[:], tf[:, :], "ld_tf")
                        S.dmaq(sy, SL, brt_sb[:], brt[:, :], "ld_brt")

                @block.vector
                def _(v, ncc=ncc):
                    for half, stg in ((0, k_f), (1, kv_f)):
                        S.wait(v, f"xs{ncc}_{half}")
                        ins = v.tensor_copy(
                            out=bass.AP(xbf, half * 4 * T + ncc * 512,
                                        [[DC * T, 128], [T, 4], [1, 512]]),
                            in_=stg[:].rearrange("p (dc t) -> p dc t", dc=4))
                        S.inc(ins, VE, f"c{ncc}_{half}")
                    S.ev[f"ld_xn{ncc}"] = S.ev[f"c{ncc}_1"]
                prev = ncc

            @block.scalar
            def _(s):
                S.wait(s, "ld_td", "ld_tf")
                ins = s.activation(out=lam8[:], in_=td_sb[:], func=AF.Exp)
                S.fence(s, ins, SC)
                s.activation(out=dcol8[:], in_=lam8[:], func=AF.Exp, scale=-1.0)
                ins = s.activation(out=eucol8[:], in_=tf_sb[:], func=AF.Exp)
                S.rbfence(s, ins, SC, "decay_done",
                          lambda rd, s=s: s.copy(out=fb_s[:, 0:8], in_=rd),
                          [lam8[:], dcol8[:], eucol8[:]])

            @block.vector
            def _(v):
                S.wait(v, "decay_done")
                ins = v.tensor_scalar(out=neglam8[:], in0=lam8[:], scalar1=-1.0,
                                      scalar2=None, op0=ALU.mult)
                S.rbfence(v, ins, VE, "neglam_done",
                          lambda rd, v=v: v.tensor_copy(out=fb_v[:, 0:8], in_=rd),
                          [neglam8[:]])


            @block.vector
            def _(v):
                v.memset(ss[:, 0:1], 0.0)

            def proj_mms(t, wtile, ec, psum, ev_pref, wait_evs, ns=DC):
                co = (ec % ns) * 128
                if ec == 0:
                    for ncc in range(NC_):
                        S.wait(t, *wait_evs(0))
                        S.wait(t, f"ld_xn{ncc}")
                        for dc in range(DC):
                            lhsT = wtile[:, dc * ns * 128 + co: dc * ns * 128 + co + 128]
                            rhs = xbf[:, dc * T + ncc * 512: dc * T + (ncc + 1) * 512]
                            ins = t.matmul(psum[:, ncc * 512:(ncc + 1) * 512],
                                           lhsT, rhs,
                                           start=(dc == 0), stop=(dc == DC - 1))
                else:
                    for dc in range(DC):
                        S.wait(t, *wait_evs(dc))
                        lhsT = wtile[:, dc * ns * 128 + co: dc * ns * 128 + co + 128]
                        for ncc in range(NC_):
                            rhs = xbf[:, dc * T + ncc * 512: dc * T + (ncc + 1) * 512]
                            ins = t.matmul(psum[:, ncc * 512:(ncc + 1) * 512],
                                           lhsT, rhs,
                                           start=(dc == 0), stop=(dc == DC - 1))
                S.inc(ins, PE, f"{ev_pref}{ec}")

            for ec in range(DC):
                pa = ps_a if ec % 2 == 0 else ps_b
                pb = ps_b if ec % 2 == 0 else ps_a

                @block.tensor
                def _(t, ec=ec, pa=pa, pb=pb):
                    evs = (lambda dc: (["ld_wk0"] if dc == 0 else [])) if ec == 0 \
                        else (lambda dc, e=ec: ([f"v_ev{e-1}", f"ld_wk{e}"]
                                                if dc == 0 else []))
                    proj_mms(t, wkb, ec, pa, "pe_k", evs)
                    evs = (lambda dc: (["ld_wv0"] if dc == 0 else [])) if ec == 0 \
                        else (lambda dc, e=ec: ([f"r_ev{e-1}", f"ld_wv{e}"]
                                                if dc == 0 else []))
                    proj_mms(t, wvb, ec, pb, "pe_v", evs, ns=4)

                @block.vector
                def _(v, ec=ec, pa=pa):
                    S.wait(v, f"pe_k{ec}")
                    if ec > 0:
                        S.wait(v, f"attn{ec-1}")
                    ins = v.tensor_copy(out=k_f[:], in_=pa[:])
                    S.inc(ins, VE, f"k_ev{ec}")

                @block.tensor
                def _(t, ec=ec, pa=pa):
                    evs = (lambda dc, e=ec: ([f"k_ev{e}", f"ld_wr{e}"]
                                             if dc == 0 else []))
                    proj_mms(t, wrb, ec, pa, "pe_r", evs, ns=4)

                @block.scalar
                def _(s, ec=ec, pa=pa, pb=pb):
                    S.wait(s, f"pe_v{ec}")
                    if ec > 0:
                        S.wait(s, f"kv{ec-1}")
                    ins = s.copy(out=v_f[:], in_=pb[:])
                    S.inc(ins, SC, f"v_ev{ec}")
                    if ec == 0:
                        S.wait(s, "consts_done", "neglam_done", "iota1p_done")
                    else:
                        S.wait(s, f"attn{ec-1}")
                    ins = s.activation(out=posd[:], in_=iota1p[:], func=AF.Exp,
                                       scale=neglam8[:, ec:ec + 1])
                    S.fence(s, ins, SC, f"posd{ec}")
                    S.wait(s, f"pe_r{ec}")
                    if ec > 0:
                        S.wait(s, f"attn{ec-1}")
                    ins = s.activation(out=r_f[:, 0:T],
                                       in_=pa[:], func=AF.Sigmoid)
                    S.inc(ins, SC, f"r_ev{ec}")

                @block.vector
                def _(v, ec=ec):
                    S.wait(v, f"v_ev{ec}", f"posd{ec}", f"k_ev{ec}")
                    ins = v.tensor_tensor(out=kv_f[:], in0=k_f[:], in1=v_f[:],
                                          op=ALU.mult)
                    S.inc(ins, VE, f"kv{ec}")
                    S.wait(v, f"kv{ec}")
                    dbc = bass.AP(dcol8, ec, [[DC, 128], [0, 1024]])
                    v.tensor_tensor_scan(out=ss[:, 1:1025], data0=dbc,
                                         data1=kv_f[:, 0:1024], initial=0.0,
                                         op0=ALU.mult, op1=ALU.add)
                    ins = v.tensor_tensor_scan(out=ss[:, 1025:T + 1], data0=dbc,
                                         data1=kv_f[:, 1024:T], initial=0.0,
                                         op0=ALU.mult, op1=ALU.add)
                    S.inc(ins, VE, f"scan{ec}")
                    S.wait(v, f"scan{ec}")
                    ins = v.scalar_tensor_tensor(out=ssc[:], in0=posd[:],
                                           scalar=ss[:, 1024:1025],
                                           in1=ss[:, 1025:T + 1],
                                           op0=ALU.mult, op1=ALU.add)
                    S.fence(v, ins, VE)
                    v.scalar_tensor_tensor(out=k_f[:, 0:1025], in0=kv_f[:, 0:1025],
                                           scalar=eucol8[:, ec:ec + 1],
                                           in1=ss[:, 0:1025],
                                           op0=ALU.mult, op1=ALU.add)
                    ins = v.scalar_tensor_tensor(out=k_f[:, 1025:T], in0=kv_f[:, 1025:T],
                                           scalar=eucol8[:, ec:ec + 1],
                                           in1=ssc[:, 0:1023],
                                           op0=ALU.mult, op1=ALU.add)
                    S.fence(v, ins, VE)
                    S.wait(v, f"r_ev{ec}")
                    ins = v.tensor_tensor(out=attn_in[:, ec * T:(ec + 1) * T],
                                          in0=k_f[:], in1=r_f[:, 0:T],
                                          op=ALU.mult)
                    S.inc(ins, VE, f"attn{ec}")

                if ec == 3:
                    @block.gpsimd
                    def _(g):
                        for dst, src, nm, pre in ((wvb, wv, "ld_wv", "pe_v3"),
                                                  (wrb, wr, "ld_wr", "pe_r3")):
                            S.wait(g, pre)
                            S.dmaq(g, GL,
                                   bass.AP(dst, 0,
                                           [[DC * 512, 128], [512, DC], [1, 512]]),
                                   src[:, 512:1024].rearrange(
                                       "(dc p) o -> p dc o", p=128), f"{nm}4")
                            for e in range(5, 8):
                                S.ev[f"{nm}{e}"] = S.ev[f"{nm}4"]

            if dbg:
                @block.sync
                def _(sy):
                    S.wait(sy, "attn7")
                    S.dmaq(sy, SL, d_kf[:, :], k_f[:], None)
                    S.dmaq(sy, SL, d_vf[:, :], v_f[:], None)
                    S.dmaq(sy, SL, d_rf[:, :], r_f[:, 0:T], None)
                    S.dmaq(sy, SL, d_xbf[:, :], xbf[:], None)
                    S.dmaq(sy, SL, d_wkb[:, :], wkb[:], None)
                    S.dmaq(sy, SL, d_ss[:, :], ss[:, 1:T + 1], None)
                    S.dmaq(sy, SL, d_attn[:, :], attn_in[:], "dbg_p1")

            @block.gpsimd
            def _(g):
                S.dmaq(g, GL, bex_bf[:], bex[:, :], "ld_bex")
                S.dmaq(g, GL, wrt_sb[:].rearrange("p (dc e) -> p dc e", dc=DC),
                       wrt[:, :, :], "ld_wrt")
                for ec in range(DC):
                    S.wait(g, f"pe_k{ec}")
                    S.dmaq(g, GL,
                           bass.AP(wkb, ec * 128, [[DC * D, 128], [D, DC], [1, 128]]),
                           wo[:, ec * 128:(ec + 1) * 128].rearrange(
                               "(dc p) o -> p dc o", p=128), f"ld_wo{ec}")

        with ExitStack() as p2:
            def p2sb(name, shape, dt=F32):
                return p2.enter_context(nc.sbuf_tensor(name, shape, dt))

            ybf = p2sb("ybf", [128, DC * T], BF16)
            xb_bf = p2sb("xb_bf", [128, 4 * T], BF16)
            sqbf = p2sb("sqbf", [128, 2 * T], BF16)
            m_all = p2sb("m_all", [1, T], BF16)
            var_all = p2sb("var_all", [1, T])
            sd_all = p2sb("sd_all", [1, T])
            rs_all = p2sb("rs_all", [1, T], BF16)
            mb_sb = p2sb("mb_sb", [128, 2 * 512], BF16)
            rb_sb = p2sb("rb_sb", [128, 2 * 512], BF16)
            x1f = p2sb("x1f", [128, DC * 512], BF16)
            ps_a2 = p2.enter_context(nc.psum_tensor("ps_a2", [128, T], F32))
            ps_b2 = p2.enter_context(nc.psum_tensor("ps_b2", [128, T], F32))


            @block.scalar
            def _(s):
                ins = s.activation(out=fb_sr[0:1, 0:1], in_=eps1[:, 0:1],
                                   func=AF.Ln)
                S.fence(s, ins, SC, "ln_warm")

            for oc in range(DC):
                psu = ps_a2 if oc % 2 == 0 else ps_b2

                @block.gpsimd
                def _(g, oc=oc):
                    if oc == 0:
                        S.wait(g, "pe_r7", "attn7")
                    if oc >= 4:
                        S.wait(g, f"y_ev{oc-4}")
                    S.dmaq(g, GL, xb_bf[:, (oc % 4) * T:(oc % 4 + 1) * T],
                           xT[oc * 128:(oc + 1) * 128, :], f"ld_xb{oc}")

                @block.tensor
                def _(t, oc=oc, psu=psu):
                    if oc >= 2:
                        S.wait(t, f"y_ev{oc-2}")
                    for dc in range(DC):
                        if dc == 0:
                            S.wait(t, f"ld_wo{oc}")
                        if oc == 0 and dc == 0:
                            S.wait(t, "r_ev7", "v_ev7")
                        if oc == 0:
                            S.wait(t, f"attn{dc}")
                        lhsT = wkb[:, dc * D + oc * 128: dc * D + (oc + 1) * 128]
                        for ncc in range(NC_):
                            rhs = attn_in[:, dc * T + ncc * 512: dc * T + (ncc + 1) * 512]
                            ins = t.matmul(psu[:, ncc * 512:(ncc + 1) * 512],
                                           lhsT, rhs,
                                           start=(dc == 0), stop=(dc == DC - 1))
                    S.inc(ins, PE, f"pe_y{oc}")

                @block.vector
                def _(v, oc=oc, psu=psu):
                    S.wait(v, f"pe_y{oc}", f"ld_xb{oc}")
                    ins = v.tensor_tensor(out=ybf[:, oc * T:(oc + 1) * T],
                                          in0=psu[:],
                                          in1=xb_bf[:, (oc % 4) * T:(oc % 4 + 1) * T],
                                          op=ALU.add)
                    S.inc(ins, VE, f"y_ev{oc}")

            @block.gpsimd
            def _(g):
                for e in range(NE):
                    S.dmaq(g, GL, wec.ap()[e, :, :], we[e, :, :], f"wec{e}")

            for dc in range(DC):
                @block.vector
                def _(v, dc=dc):
                    S.wait(v, f"y_ev{dc}")
                    if dc >= 2:
                        S.wait(v, f"pe_sq{dc-2}")
                    ins = v.tensor_tensor(out=sqbf[:, (dc % 2) * T:(dc % 2 + 1) * T],
                                          in0=ybf[:, dc * T:(dc + 1) * T],
                                          in1=ybf[:, dc * T:(dc + 1) * T],
                                          op=ALU.mult)
                    S.inc(ins, VE, f"sq{dc}")

                @block.tensor
                def _(t, dc=dc):
                    if dc == 0:
                        S.wait(t, "y_ev6", "y_ev7", "consts_done")
                    S.wait(t, f"sq{dc}")
                    for ncc in range(NC_):
                        t.matmul(ps_a2[0:1, ncc * 512:(ncc + 1) * 512], ones_bf[:, :],
                                 ybf[:, dc * T + ncc * 512: dc * T + (ncc + 1) * 512],
                                 start=(dc == 0), stop=(dc == DC - 1))
                        ins = t.matmul(ps_b2[0:1, ncc * 512:(ncc + 1) * 512], ones_bf[:, :],
                                       sqbf[:, (dc % 2) * T + ncc * 512:(dc % 2) * T + (ncc + 1) * 512],
                                       start=(dc == 0), stop=(dc == DC - 1))
                    S.inc(ins, PE, f"pe_sq{dc}")

            def emit_q(h):
                @block.vector
                def _(v, h=h):
                    hs = slice(h * 512, (h + 1) * 512)
                    if h == 0:
                        S.wait(v, "pe_sq7")

                    def edge(ins, tag, v=v):
                        S.inc(ins, VE, tag)
                        S.wait(v, tag)
                    edge(v.tensor_scalar(out=m_all[0:1, hs], in0=ps_a2[0:1, hs],
                                         scalar1=1.0 / D, scalar2=None,
                                         op0=ALU.mult), f"ln1_m{h}")
                    edge(v.tensor_tensor(out=sd_all[0:1, hs], in0=m_all[0:1, hs],
                                         in1=m_all[0:1, hs], op=ALU.mult),
                         f"ln1_s{h}")
                    ins = v.scalar_tensor_tensor(out=var_all[0:1, hs],
                                                 in0=ps_b2[0:1, hs],
                                                 scalar=1.0 / D,
                                                 in1=sd_all[0:1, hs],
                                                 op0=ALU.mult, op1=ALU.subtract)
                    S.rbfence(v, ins, VE, f"ln1_var{h}",
                              lambda rd, v=v: v.tensor_copy(out=fb_vr[:], in_=rd),
                              [var_all[0:1, h * 512 + 384:h * 512 + 512],
                               m_all[0:1, h * 512 + 384:h * 512 + 512]])

                @block.scalar
                def _(s, h=h):
                    hs = slice(h * 512, (h + 1) * 512)
                    S.wait(s, f"ln1_var{h}")
                    ins = s.activation(out=sd_all[0:1, hs], in_=var_all[0:1, hs],
                                       func=AF.Ln, bias=eps1[:, 0:1])
                    S.inc(ins, SC, f"ln1_ln{h}")
                    s.wait_ge(S.sems[SC], S.cnt[SC])
                    ins = s.activation(out=rs_all[0:1, hs], in_=sd_all[0:1, hs],
                                       func=AF.Exp, scale=-0.5)
                    S.rbfence(s, ins, SC, f"ln1_rs{h}",
                              lambda rd, s=s: s.copy(out=fb_sr[:], in_=rd),
                              [rs_all[0:1, h * 512 + 384:h * 512 + 512]])

            emit_q(0)
            emit_q(1)
            emit_q(2)
            emit_q(3)

            for ncc in range(NC_):
                @block.tensor
                def _(t, ncc=ncc):
                    cs = slice(ncc * 512, (ncc + 1) * 512)
                    S.wait(t, f"ln1_rs{ncc}", f"ln1_var{ncc}")
                    if ncc < 2:
                        S.wait(t, f"ln1_m{ncc + 2}")
                    if ncc > 0:
                        S.wait(t, f"mb_ev{ncc-1}", f"rb_ev{ncc-1}")
                    b = ncc % 2
                    ins = t.matmul(ps_a2[:, b * 512:(b + 1) * 512], ones_bf1[:, :],
                                   m_all[0:1, cs], start=True, stop=True)
                    S.inc(ins, PE, f"pe_mb{ncc}")
                    ins = t.matmul(ps_a2[:, 1024 + b * 512: 1024 + (b + 1) * 512],
                                   ones_bf1[:, :], rs_all[0:1, cs], start=True, stop=True)
                    S.inc(ins, PE, f"pe_rb{ncc}")

                @block.scalar
                def _(s, ncc=ncc):
                    b = ncc % 2
                    S.wait(s, f"pe_mb{ncc}")
                    ins = s.copy(out=mb_sb[:, b * 512:(b + 1) * 512],
                                 in_=ps_a2[:, b * 512:(b + 1) * 512])
                    S.inc(ins, SC, f"mb_ev{ncc}")
                    S.wait(s, f"pe_rb{ncc}")
                    ins = s.copy(out=rb_sb[:, b * 512:(b + 1) * 512],
                                 in_=ps_a2[:, 1024 + b * 512: 1024 + (b + 1) * 512])
                    S.inc(ins, SC, f"rb_ev{ncc}")

                @block.vector
                def _(v, ncc=ncc):
                    b = ncc % 2
                    S.wait(v, f"mb_ev{ncc}", f"rb_ev{ncc}")
                    if ncc > 0:
                        S.wait(v, f"x1f{ncc-1}_{DC-1}")
                    for dp in range(0, DC, 2):
                        for dc in (dp, dp + 1):
                            ins = v.tensor_tensor(
                                out=x1f[:, dc * 512:(dc + 1) * 512],
                                in0=ybf[:, dc * T + ncc * 512: dc * T + (ncc + 1) * 512],
                                in1=mb_sb[:, b * 512:(b + 1) * 512],
                                op=ALU.subtract)
                        S.fence(v, ins, VE)
                        for dc in (dp, dp + 1):
                            ins = v.tensor_tensor(
                                out=x1bf[:, dc * T + ncc * 512: dc * T + (ncc + 1) * 512],
                                in0=x1f[:, dc * 512:(dc + 1) * 512],
                                in1=rb_sb[:, b * 512:(b + 1) * 512],
                                op=ALU.mult)
                            S.inc(ins, VE, f"x1f{ncc}_{dc}")
                    if ncc == NC_ - 1:
                        S.ev["x1c3_7"] = S.ev[f"x1f{ncc}_{DC-1}"]

                for dc in range(DC):
                    @block.tensor
                    def _(t, ncc=ncc, dc=dc):
                        b = ncc % 2
                        S.wait(t, f"x1f{ncc}_{dc}")
                        if ncc == 0 and dc == 0:
                            S.wait(t, "ld_wrt")
                        if ncc >= 2 and dc == 0:
                            S.wait(t, f"lg_ev{ncc-2}")
                        ins = t.matmul(ps_b2[0:8, b * 512:(b + 1) * 512],
                                       wrt_sb[:, dc * 8:(dc + 1) * 8],
                                       x1bf[:, dc * T + ncc * 512: dc * T + (ncc + 1) * 512],
                                       start=(dc == 0), stop=(dc == DC - 1))
                        S.inc(ins, PE, f"rt{ncc}_{dc}")

                @block.vector
                def _(v, ncc=ncc):
                    b = ncc % 2
                    S.wait(v, f"rt{ncc}_7", "ld_brt")
                    ins = v.tensor_scalar(out=lg_sb[:, ncc * 512:(ncc + 1) * 512],
                                          in0=ps_b2[0:8, b * 512:(b + 1) * 512],
                                          scalar1=brt_sb[:, 0:1], scalar2=None,
                                          op0=ALU.add)
                    S.rbfence(v, ins, VE, f"lg_ev{ncc}",
                              lambda rd, v=v: v.tensor_copy(out=fb_v[0:8, 0:8], in_=rd),
                              [lg_sb[0:8, (ncc + 1) * 512 - 8:(ncc + 1) * 512]])

            if dbg:
                @block.sync
                def _(sy):
                    S.wait(sy, *[f"y_ev{o}" for o in range(DC)])
                    S.dmaq(sy, SL, d_ybf[:, :], ybf[:], None)
                    S.wait(sy, *[f"lg_ev{n}" for n in range(NC_)])
                    S.dmaq(sy, SL, d_lg[:, :], lg_sb[:], None)
                    S.wait(sy, "x1c3_7")
                    S.dmaq(sy, SL, d_x1[:, :], x1bf[:], "dbg_p2")

        pA.close()

        with ExitStack() as p34:
            def p4sb(name, shape, dt=F32):
                return p34.enter_context(nc.sbuf_tensor(name, shape, dt))

            WSLB = 3
            XEB = 3
            wsl = p4sb("wsl", [128, WSLB * DC * D], BF16)
            xe = p4sb("xe", [128, XEB * DC * TQ], BF16)
            wb_all = p4sb("wb_all", [128, NE * T], BF16)
            y2bf = p4sb("y2bf", [128, DC * TQ], BF16)
            sq2 = p4sb("sq2", [128, DC * TQ], BF16)
            m2_row = p4sb("m2_row", [1, TQ], BF16)
            s2_row = p4sb("s2_row", [1, TQ])
            v2_row = p4sb("v2_row", [1, TQ])
            r2_row = p4sb("r2_row", [1, TQ], BF16)
            mb2 = p4sb("mb2", [128, TQ], BF16)
            rb2s = p4sb("rb2s", [128, TQ], BF16)
            pisl = ExitStack()
            def pislsb(name, shape, dt=F32):
                return pisl.enter_context(nc.sbuf_tensor(name, shape, dt))
            lgT = pislsb("lgT", [128, 128])
            wtt = pislsb("wtt", [128, 128])
            m1_16 = pislsb("m1_16", [128, 16])
            m2_16 = pislsb("m2_16", [128, 16])
            den16 = pislsb("den16", [128, 16])
            rec16 = pislsb("rec16", [128, 16])
            eqx = pislsb("eqx", [128, 128])
            mskx = pislsb("mskx", [128, 128])
            e1x = pislsb("e1x", [128, 128])
            expx = pislsb("expx", [128, 128])
            keepx = pislsb("keepx", [128, 128])
            ps_f = [p34.enter_context(nc.psum_tensor(f"ps_f{i}", [128, 512], F32))
                    for i in range(8)]

            lgT3 = lgT[:].rearrange("p (i e) -> p i e", i=16)
            mskx3 = mskx[:].rearrange("p (i e) -> p i e", i=16)
            expx3 = expx[:].rearrange("p (i e) -> p i e", i=16)
            m1b = bass.AP(m1_16, 0, [[16, 128], [1, 16], [0, 8]])
            m2b = bass.AP(m2_16, 0, [[16, 128], [1, 16], [0, 8]])
            recb = bass.AP(rec16, 0, [[16, 128], [1, 16], [0, 8]])

            for i in range(16):
                bk = 4 + (i % 2)

                @block.tensor
                def _(t, i=i, bk=bk):
                    S.wait(t, f"lg_ev{i//4}")
                    if i == 0:
                        S.wait(t, "mb_ev3", "rb_ev3", "lg_ev3", "x1c3_7")
                    if i >= 2:
                        S.wait(t, f"lgT_ev{i-2}")
                    ins = t.transpose(ps_f[bk][:, 0:8],
                                      lg_sb[0:8, i * 128:(i + 1) * 128], id_f8[:, :])
                    S.inc(ins, PE, f"pe_tp{i}")

                @block.scalar
                def _(s, i=i, bk=bk):
                    S.wait(s, f"pe_tp{i}")
                    if i == 0:
                        S.wait(s, "x1c3_7")
                        if dbg:
                            S.wait(s, "dbg_p2")
                    ins = s.copy(out=lgT[:, i * 8:(i + 1) * 8], in_=ps_f[bk][:, 0:8])
                    if i in (7, 15):
                        S.rbfence(s, ins, SC, f"lgT_ev{i}",
                                  lambda rd, s=s, i=i: s.copy(out=fb_s[:, 8:16], in_=rd),
                                  [lgT[:, i * 8:(i + 1) * 8]])
                    else:
                        S.inc(ins, SC, f"lgT_ev{i}")

            for h in range(2):
                hs = slice(h * 64, (h + 1) * 64)
                h8 = slice(h * 8, (h + 1) * 8)

                @block.vector
                def _(v, h=h, hs=hs, h8=h8):
                    lgT3h = lgT[:, hs].rearrange("p (i e) -> p i e", i=8)
                    mskx3h = mskx[:, hs].rearrange("p (i e) -> p i e", i=8)
                    m1bh = bass.AP(m1_16, h * 8, [[16, 128], [1, 8], [0, 8]])
                    S.wait(v, f"lgT_ev{h * 8 + 7}")
                    if h == 0:
                        S.wait(v, "x1c3_7")
                        if dbg:
                            S.wait(v, "dbg_p2")

                    def edge(ins, tag, v=v):
                        S.inc(ins, VE, tag)
                        S.wait(v, tag)
                    edge(v.reduce_max(out=m1_16[:, h8], in_=lgT3h, axis=AX.X),
                         f"isl_m1{h}")
                    ins = v.tensor_tensor(
                        out=e1x[:, hs].rearrange("p (i e) -> p i e", i=8),
                        in0=lgT3h, in1=m1bh, op=ALU.subtract)
                    S.rbfence(v, ins, VE, f"isl_e1{h}",
                              lambda rd, v=v: v.tensor_copy(out=fb_v[:, 0:8], in_=rd),
                              [e1x[:, h * 64 + 56:h * 64 + 64]])
                    edge(v.tensor_tensor(
                        out=eqx[:, hs].rearrange("p (i e) -> p i e", i=8),
                        in0=lgT3h, in1=m1bh, op=ALU.is_equal), f"isl_eq{h}")
                    edge(v.scalar_tensor_tensor(out=mskx[:, hs], in0=eqx[:, hs],
                                                scalar=-1e30, in1=lgT[:, hs],
                                                op0=ALU.mult, op1=ALU.add),
                         f"isl_msk{h}")
                    edge(v.reduce_max(out=m2_16[:, h8], in_=mskx3h, axis=AX.X),
                         f"isl_m2{h}")
                    edge(v.tensor_tensor(
                        out=keepx[:, hs].rearrange("p (i e) -> p i e", i=8),
                        in0=lgT3h,
                        in1=bass.AP(m2_16, h * 8, [[16, 128], [1, 8], [0, 8]]),
                        op=ALU.is_ge), f"isl_keep{h}")

            for h in range(2):
                hs = slice(h * 64, (h + 1) * 64)

                @block.scalar
                def _(s, h=h, hs=hs):
                    S.wait(s, f"isl_e1{h}")
                    ins = s.activation(out=expx[:, hs], in_=e1x[:, hs], func=AF.Exp)
                    S.rbfence(s, ins, SC, f"isl_exp{h}",
                              lambda rd, s=s: s.copy(out=fb_s[:, 8:16], in_=rd),
                              [expx[:, h * 64 + 56:h * 64 + 64]])

            for h in range(2):
                hs = slice(h * 64, (h + 1) * 64)
                h8 = slice(h * 8, (h + 1) * 8)

                @block.vector
                def _(v, h=h, hs=hs, h8=h8):
                    expx3h = expx[:, hs].rearrange("p (i e) -> p i e", i=8)
                    recbh = bass.AP(rec16, h * 8, [[16, 128], [1, 8], [0, 8]])
                    S.wait(v, f"isl_exp{h}")

                    def edge(ins, tag, v=v):
                        S.inc(ins, VE, tag)
                        S.wait(v, tag)
                    edge(v.reduce_sum(out=den16[:, h8], in_=expx3h, axis=AX.X),
                         f"isl_den{h}")
                    edge(v.reciprocal(out=rec16[:, h8], in_=den16[:, h8]),
                         f"isl_rec{h}")
                    edge(v.tensor_tensor(
                        out=e1x[:, hs].rearrange("p (i e) -> p i e", i=8),
                        in0=expx3h, in1=recbh, op=ALU.mult), f"isl_t2{h}")
                    ins = v.tensor_tensor(out=wtt[:, hs], in0=e1x[:, hs],
                                          in1=keepx[:, hs], op=ALU.mult)
                    S.rbfence(v, ins, VE, f"isl_w{h}",
                              lambda rd, v=v: v.tensor_copy(out=fb_v[:, 0:8], in_=rd),
                              [wtt[:, h * 64 + 56:h * 64 + 64]])

            for i in range(16):
                bk = 6 + (i % 2)

                @block.tensor
                def _(t, i=i, bk=bk):
                    S.wait(t, f"isl_w{i // 8}")
                    if i >= 2:
                        S.wait(t, f"wfm_ev{i-2}")
                    ins = t.transpose(ps_f[bk][0:8, 0:128],
                                      wtt[:, i * 8:(i + 1) * 8], id_f[:, :])
                    S.inc(ins, PE, f"pe_tw{i}")

                @block.scalar
                def _(s, i=i, bk=bk):
                    S.wait(s, f"pe_tw{i}")
                    ins = s.copy(out=wfm[0:8, i * 128:(i + 1) * 128],
                                 in_=ps_f[bk][0:8, 0:128])
                    if i % 4 == 3:
                        S.rbfence(s, ins, SC, f"wfm_ev{i}",
                                  lambda rd, s=s: s.copy(out=fb_s[0:8, 0:16], in_=rd),
                                  [wfm[0:8, (i + 1) * 128 - 16:(i + 1) * 128]])
                    else:
                        S.inc(ins, SC, f"wfm_ev{i}")

            for qq in range(4):
                for e in range(NE):
                    bk2 = (qq * NE + e) % 4

                    @block.tensor
                    def _(t, qq=qq, e=e, bk2=bk2):
                        S.wait(t, f"wfm_ev{qq * 4 + 3}")
                        gidx = qq * NE + e
                        if gidx >= 4:
                            S.wait(t, f"wbev{(gidx-4)//NE}_{(gidx-4)%NE}")
                        ins = t.matmul(ps_f[bk2][:, :],
                                       sel_bf[:, e * 128:(e + 1) * 128],
                                       wfm[0:8, qq * TQ:(qq + 1) * TQ],
                                       start=True, stop=True)
                        S.inc(ins, PE, f"pe_wb{qq}_{e}")

                    @block.vector
                    def _(v, qq=qq, e=e, bk2=bk2):
                        S.wait(v, f"pe_wb{qq}_{e}")
                        ins = v.tensor_copy(
                            out=wb_all[:, e * T + qq * TQ: e * T + (qq + 1) * TQ],
                            in_=ps_f[bk2][:, :])
                        S.inc(ins, VE, f"wbev{qq}_{e}")

            if dbg:
                @block.sync
                def _(sy):
                    S.wait(sy, *[f"wfm_ev{i}" for i in range(16)])
                    S.dmaq(sy, SL, d_wfm[:, :], wfm[:], "dbg_p3")

            pisl.close()
            outr = p4sb("outr", [128, 4 * TQ])
            otmp = p4sb("otmp", [128, 2 * TQ], BF16)

            OCROT = [4, 5, 6, 7, 2, 3, 0, 1]

            def emit_ln2_tail_a(q):
                for oc in range(NE):
                    @block.tensor
                    def _(t, q=q, oc=oc):
                        S.wait(t, f"sq2{q}_{oc}")
                        if oc == 0:
                            S.wait(t, f"y2_ev{q}_0", f"y2_ev{q}_1")
                        t.matmul(ps_f[0][0:1, :], ones_bf[:, :],
                                 y2bf[:, oc * TQ:(oc + 1) * TQ],
                                 start=(oc == 0), stop=(oc == NE - 1))
                        ins = t.matmul(ps_f[1][0:1, :], ones_bf[:, :],
                                       sq2[:, oc * TQ:(oc + 1) * TQ],
                                       start=(oc == 0), stop=(oc == NE - 1))
                        S.inc(ins, PE, f"st2{q}_{oc}")

                @block.vector
                def _(v, q=q):
                    S.wait(v, f"st2{q}_{NE-1}")

                    def edge(ins, tag, v=v, q=q):
                        S.inc(ins, VE, f"{tag}e{q}")
                        S.wait(v, f"{tag}e{q}")
                    edge(v.tensor_scalar(out=m2_row[:], in0=ps_f[0][0:1, :],
                                         scalar1=1.0 / D, scalar2=None,
                                         op0=ALU.mult), "m2")
                    edge(v.tensor_tensor(out=s2_row[:], in0=m2_row[:], in1=m2_row[:],
                                         op=ALU.mult), "s2")
                    ins = v.scalar_tensor_tensor(out=v2_row[:], in0=ps_f[1][0:1, :],
                                                 scalar=1.0 / D, in1=s2_row[:],
                                                 op0=ALU.mult, op1=ALU.subtract)
                    S.rbfence(v, ins, VE, f"st_ev{q}",
                              lambda rd, v=v: v.tensor_copy(out=fb_vr[:], in_=rd),
                              [v2_row[0:1, 384:512], m2_row[0:1, 384:512]])

                @block.scalar
                def _(s, q=q):
                    S.wait(s, f"st_ev{q}")
                    ins = s.activation(out=s2_row[:], in_=v2_row[:], func=AF.Ln,
                                       bias=eps1[:, 0:1])
                    S.inc(ins, SC, f"ln2l{q}")
                    s.wait_ge(S.sems[SC], S.cnt[SC])
                    ins = s.activation(out=r2_row[:], in_=s2_row[:], func=AF.Exp,
                                       scale=-0.5)
                    S.rbfence(s, ins, SC, f"rs2_ev{q}",
                              lambda rd, s=s: s.copy(out=fb_sr[:], in_=rd),
                              [r2_row[0:1, 384:512]])

            def emit_ln2_bcast(q):
                @block.tensor
                def _(t, q=q):
                    S.wait(t, f"rs2_ev{q}", f"st_ev{q}")
                    ins = t.matmul(ps_f[2][:, :], ones_bf1[:, :], m2_row[0:1, :],
                                   start=True, stop=True)
                    S.inc(ins, PE, f"pe_m2{q}")
                    ins = t.matmul(ps_f[3][:, :], ones_bf1[:, :], r2_row[0:1, :],
                                   start=True, stop=True)
                    S.inc(ins, PE, f"pe_r2{q}")

                @block.scalar
                def _(s, q=q):
                    S.wait(s, f"pe_m2{q}")
                    if q > 0:
                        S.wait(s, f"o_ev{q-1}_{NE-1}")
                    ins = s.copy(out=mb2[:], in_=ps_f[2][:, :])
                    S.inc(ins, SC, f"mb2_ev{q}")
                    S.wait(s, f"pe_r2{q}")
                    ins = s.copy(out=rb2s[:], in_=ps_f[3][:, :])
                    S.inc(ins, SC, f"rb2_ev{q}")

            def emit_ln2_apply(q):
                qs = slice(q * TQ, (q + 1) * TQ)
                for oc in range(NE):
                    gi = q * NE + oc
                    ti = gi % 2
                    ri = gi % 4

                    @block.vector
                    def _(v, q=q, oc=oc, ti=ti, gi=gi):
                        S.wait(v, f"mb2_ev{q}", f"rb2_ev{q}")
                        if gi >= 2:
                            S.wait(v, f"o_ev{(gi-2)//NE}_{(gi-2)%NE}")
                        ins = v.tensor_tensor(out=otmp[:, ti * TQ:(ti + 1) * TQ],
                                              in0=y2bf[:, oc * TQ:(oc + 1) * TQ],
                                              in1=mb2[:], op=ALU.subtract)
                        S.inc(ins, VE, f"o1_{q}_{oc}")

                    @block.vector
                    def _(v, q=q, oc=oc, ti=ti, ri=ri, gi=gi):
                        S.wait(v, f"o1_{q}_{oc}")
                        if gi >= 4:
                            S.wait(v, f"od{(gi-4)//NE}_{(gi-4)%NE}")
                        ins = v.tensor_tensor(out=outr[:, ri * TQ:(ri + 1) * TQ],
                                              in0=otmp[:, ti * TQ:(ti + 1) * TQ],
                                              in1=rb2s[:], op=ALU.mult)
                        S.inc(ins, VE, f"o_ev{q}_{oc}")

                    if oc % 2 == 0:
                        @block.scalar
                        def _(sc2, q=q, oc=oc, ri=ri):
                            S.wait(sc2, f"o_ev{q}_{oc}")
                            S.dmaq(sc2, OU, out[oc * 128:(oc + 1) * 128, qs],
                                   outr[:, ri * TQ:(ri + 1) * TQ], f"od{q}_{oc}")
                    else:
                        @block.sync
                        def _(sy, q=q, oc=oc, ri=ri):
                            S.wait(sy, f"o_ev{q}_{oc}")
                            S.dmaq(sy, SL, out[oc * 128:(oc + 1) * 128, qs],
                                   outr[:, ri * TQ:(ri + 1) * TQ], f"od{q}_{oc}")

            for q in range(4):
                qs = slice(q * TQ, (q + 1) * TQ)

                def emit_wsl(q, e):
                    eb = (q * NE + e) % WSLB

                    @block.sync
                    def _(sy, q=q, e=e, eb=eb):
                        if q == 0:
                            S.wait(sy, f"wec{e}")
                        if q == 0 and e < WSLB:
                            S.wait(sy, "x1c3_7", "lg_ev3", "rt3_7")
                            if dbg:
                                S.wait(sy, "dbg_p2")
                        if q * NE + e >= WSLB:
                            pe_i = (q * NE + e) - WSLB
                            S.wait(sy, f"ffn{pe_i//NE}_{pe_i%NE}")
                        S.dmaq(sy, SL,
                               wsl[:, eb * DC * D:(eb + 1) * DC * D].rearrange(
                                   "p (dc o) -> p dc o", dc=DC),
                               wec.ap()[e, :, :].rearrange("(dc p) o -> p dc o", p=128),
                               f"wsl{q}_{e}")

                def emit_xe(q, e):
                    xb = (q * NE + e) % XEB

                    @block.vector
                    def _(v, q=q, e=e, xb=xb):
                        S.wait(v, f"wbev{q}_{e}")
                        if q * NE + e >= XEB:
                            pe_i = (q * NE + e) - XEB
                            S.wait(v, f"ffn{pe_i//NE}_{pe_i%NE}")
                        for dc in range(DC):
                            ins = v.tensor_tensor(
                                out=xe[:, (xb * DC + dc) * TQ:(xb * DC + dc + 1) * TQ],
                                in0=x1bf[:, dc * T + q * TQ: dc * T + q * TQ + TQ],
                                in1=wb_all[:, e * T + q * TQ: e * T + (q + 1) * TQ],
                                op=ALU.mult)
                        S.inc(ins, VE, f"xe{q}_{e}")

                def ffn_mms(t, q, e, eb, xb, ocs, ocw=None):
                    ins = None
                    for oc in ocs:
                        if ocw is not None:
                            S.wait(t, *ocw(oc))
                        if e == 0 and q == 0 and oc in (0, 1, 2, 3):
                            S.wait(t, f"wbev{3}_{oc+4}")
                        for dc in range(DC):
                            lhsT = wsl[:, (eb * DC + dc) * D + oc * 128:
                                       (eb * DC + dc) * D + (oc + 1) * 128]
                            ins = t.matmul(ps_f[oc][:, :], lhsT,
                                           xe[:, (xb * DC + dc) * TQ:(xb * DC + dc + 1) * TQ],
                                           start=(e == 0 and dc == 0), stop=False)
                    return ins

                if q == 0:
                    for e in range(NE):
                        emit_wsl(q, e)
                        emit_xe(q, e)

                        @block.tensor
                        def _(t, q=q, e=e):
                            eb = (q * NE + e) % WSLB
                            xb = (q * NE + e) % XEB
                            S.wait(t, f"xe{q}_{e}", f"wsl{q}_{e}")
                            if e == 0:
                                S.wait(t, "lgT_ev15", "wfm_ev15")
                            ins = ffn_mms(t, q, e, eb, xb, OCROT)
                            S.inc(ins, PE, f"ffn{q}_{e}")
                else:
                    NA = 3
                    for e in range(NA):
                        if e >= 2:
                            emit_wsl(q, e)
                            emit_xe(q, e)

                        @block.tensor
                        def _(t, q=q, e=e):
                            eb = (q * NE + e) % WSLB
                            xb = (q * NE + e) % XEB
                            S.wait(t, f"xe{q}_{e}", f"wsl{q}_{e}")
                            ocw = None
                            if e == 0:
                                ocw = lambda oc, q=q: [f"y2_ev{q-1}_{oc}"]
                            ffn_mms(t, q, e, eb, xb, [4, 5, 6, 7], ocw=ocw)

                    emit_ln2_tail_a(q - 1)

                    @block.tensor
                    def _(t, q=q, NA=NA):
                        S.wait(t, f"st_ev{q-1}")
                        eb = (q * NE) % WSLB
                        xb = (q * NE) % XEB
                        ffn_mms(t, q, 0, eb, xb, [0, 1])

                    emit_ln2_bcast(q - 1)

                    @block.tensor
                    def _(t, q=q, NA=NA):
                        S.wait(t, f"mb2_ev{q-1}", f"rb2_ev{q-1}")
                        for e in range(NA):
                            eb = (q * NE + e) % WSLB
                            xb = (q * NE + e) % XEB
                            if e > 0:
                                ffn_mms(t, q, e, eb, xb, [0, 1])
                            ins = ffn_mms(t, q, e, eb, xb, [2, 3])
                            S.inc(ins, PE, f"ffn{q}_{e}")

                    def expert_sec(e):
                        @block.tensor
                        def _(t, q=q, e=e):
                            eb = (q * NE + e) % WSLB
                            xb = (q * NE + e) % XEB
                            S.wait(t, f"xe{q}_{e}", f"wsl{q}_{e}")
                            ins = ffn_mms(t, q, e, eb, xb, OCROT)
                            S.inc(ins, PE, f"ffn{q}_{e}")

                    for e in range(NA, NA + 3):
                        emit_wsl(q, e)
                        emit_xe(q, e)
                    expert_sec(NA)
                    expert_sec(NA + 1)
                    for e in range(NA + 3, NE):
                        emit_wsl(q, e)
                        emit_xe(q, e)

                    emit_ln2_apply(q - 1)

                    for e in range(NA + 2, NE):
                        expert_sec(e)

                if q < 3:
                    for e in range(2):
                        emit_wsl(q + 1, e)
                        emit_xe(q + 1, e)

                for oc in ((4, 5, 6, 7, 0, 1, 2, 3) if q < 3 else
                           (0, 1, 2, 3, 4, 5, 6, 7)):
                    @block.tensor
                    def _(t, q=q, oc=oc):
                        if q == 0 and oc == 0:
                            S.wait(t, "ld_bex")
                        ins = t.matmul(ps_f[oc][:, :], bex_bf[:, oc * 128:(oc + 1) * 128],
                                       wfm[0:8, qs], start=False, stop=True)
                        S.inc(ins, PE, f"bia{q}_{oc}")

                    @block.vector
                    def _(v, q=q, oc=oc):
                        S.wait(v, f"bia{q}_{oc}")
                        if dbg and q == 1 and oc == 0:
                            S.wait(v, "dbg_p4")
                        if q > 0:
                            S.wait(v, f"st2{q-1}_{oc}")
                        ins = v.tensor_tensor(out=y2bf[:, oc * TQ:(oc + 1) * TQ],
                                              in0=ps_f[oc][:, :],
                                              in1=x1bf[:, oc * T + q * TQ: oc * T + q * TQ + TQ],
                                              op=ALU.add)
                        S.inc(ins, VE, f"y2_ev{q}_{oc}")

                    @block.vector
                    def _(v2e, q=q, oc=oc):
                        S.wait(v2e, f"y2_ev{q}_{oc}")
                        if q > 0:
                            S.wait(v2e, f"st2{q-1}_{oc}")
                        ins = v2e.tensor_tensor(out=sq2[:, oc * TQ:(oc + 1) * TQ],
                                                in0=y2bf[:, oc * TQ:(oc + 1) * TQ],
                                                in1=y2bf[:, oc * TQ:(oc + 1) * TQ],
                                                op=ALU.mult)
                        S.inc(ins, VE, f"sq2{q}_{oc}")

                if dbg and q == 0:
                    @block.sync
                    def _(sy):
                        S.wait(sy, f"wbev0_{NE-1}")
                        S.dmaq(sy, SL, d_wb[:, :],
                               wb_all[:].rearrange("p (e t) -> p e t", e=NE)[:, :, 0:TQ],
                               None)
                        S.wait(sy, *[f"y2_ev0_{o}" for o in range(NE)])
                        S.dmaq(sy, SL, d_y2[:, :], y2bf[:], "dbg_p4")

            emit_ln2_tail_a(3)
            emit_ln2_bcast(3)
            emit_ln2_apply(3)

            @block.sync
            def _(sy):
                for qn in (OU, OU + "1", OU + "2", SL, SL + "1", SL + "2"):
                    if S.val(qn) > 0:
                        sy.wait_ge(S.sems[qn], S.val(qn))

    return nc


_NC_CACHE = None


def kernel(L0_attn_x, L0_attn_Wk, L0_attn_Wv, L0_attn_Wr, L0_attn_Wo,
           L0_attn_time_decay, L0_attn_time_first,
           L0_ffn_W_experts, L0_ffn_b_experts, L0_ffn_W_router, L0_ffn_b_router):
    global _NC_CACHE
    B = L0_attn_x.shape[0]
    assert L0_attn_x.shape == (8, T, D)

    wkT = np.ascontiguousarray(np.asarray(L0_attn_Wk, np.float32).T)
    wvT = np.ascontiguousarray(np.asarray(L0_attn_Wv, np.float32).T)
    wrT = np.ascontiguousarray(np.asarray(L0_attn_Wr, np.float32).T)
    woT = np.ascontiguousarray(np.asarray(L0_attn_Wo, np.float32).T)
    weT = np.ascontiguousarray(np.asarray(L0_ffn_W_experts, np.float32).transpose(0, 2, 1))
    wrtH = np.ascontiguousarray(
        np.asarray(L0_ffn_W_router, np.float32).T.reshape(DC, 128, NE).transpose(1, 0, 2))
    brtH = np.ascontiguousarray(np.asarray(L0_ffn_b_router, np.float32).reshape(NE, 1))
    bexH = np.ascontiguousarray(np.asarray(L0_ffn_b_experts, np.float32))
    tdH = np.ascontiguousarray(np.asarray(L0_attn_time_decay, np.float32).reshape(DC, 128).T)
    tfH = np.ascontiguousarray(np.asarray(L0_attn_time_first, np.float32).reshape(DC, 128).T)

    x = np.asarray(L0_attn_x, np.float32)
    in_maps = []
    for b in range(B):
        in_maps.append({
            "xT": np.ascontiguousarray(x[b].T),
            "wk": wkT, "wv": wvT, "wr": wrT, "wo": woT, "we": weT,
            "wrt": wrtH, "brt": brtH, "bex": bexH, "td": tdH, "tf": tfH,
        })

    if _NC_CACHE is None:
        _NC_CACHE = build_nc()
    res = run_bass_kernel_spmd(_NC_CACHE, in_maps, core_ids=list(range(8)))
    outs = np.stack([np.asarray(r["out"], np.float32).T for r in res.results], axis=0)
    return outs
